# version 20 (final)
# baseline (speedup 1.0000x reference)
import numpy as np
import ml_dtypes
from contextlib import ExitStack

import concourse.bass as bass
import concourse.mybir as mybir
from concourse.bass_utils import run_bass_kernel_spmd

F32 = mybir.dt.float32
BF16 = mybir.dt.bfloat16
AF = mybir.ActivationFunctionType
ALU = mybir.AluOpType
AX = mybir.AxisListType
BF = ml_dtypes.bfloat16

T = 2048
D = 1024
NT = 16
NC8 = 8
H = 16
FF = 2048
NFF = 16
NE = 17
QT = 512
NQ = 4

ENGS = ("pe", "dv", "ac", "gq", "sq")
BLOCK_OF = {"pe": "tensor", "dv": "vector", "ac": "scalar", "gq": "gpsimd", "sq": "sync"}


class Prog:

    def __init__(self):
        self.cmds = {e: [] for e in ENGS}
        self.cnt = dict.fromkeys(ENGS, 0)
        self.hwm = {e: {} for e in ENGS}
        self.res = {}

    def _wait(self, eng, tok):
        p, v = tok
        if p == eng == "pe":
            return
        if v <= self.hwm[eng].get(p, 0):
            return
        self.hwm[eng][p] = v
        self.cmds[eng].append(("w", p, v))

    def op(self, eng, fn, reads=(), writes=(), inc=1):
        toks = []
        for r in reads:
            e = self.res.get(r)
            if e is not None and e[0] is not None:
                toks.append(e[0])
        for w in writes:
            e = self.res.get(w)
            if e is not None:
                if e[0] is not None:
                    toks.append(e[0])
                toks.extend(e[1])
        for t in sorted(set(toks)):
            self._wait(eng, t)
        self.cnt[eng] += inc
        tok = (eng, self.cnt[eng])
        self.cmds[eng].append(("e", fn, inc))
        if inc == 16:
            self.hwm[eng][eng] = self.cnt[eng]
            self.cmds[eng].append(("w", eng, self.cnt[eng]))
        for r in reads:
            self.res.setdefault(r, [None, []])[1].append(tok)
        for w in writes:
            self.res[w] = [tok, []]
        return tok

    def barrier(self):
        toks = [(e, self.cnt[e]) for e in ENGS if self.cnt[e] > 0]
        for e in ENGS:
            for t in toks:
                if t[0] != e:
                    self._wait(e, t)

    def final_wait(self, eng):
        self.cmds[eng].append(("w", eng, self.cnt[eng]))

    def replay(self, nc, sems):
        with nc.Block() as block:
            for eng in ENGS:
                def body(E, eng=eng):
                    for c in self.cmds[eng]:
                        if c[0] == "w":
                            E.wait_ge(sems[c[1]], c[2])
                        else:
                            c[1](E).then_inc(sems[eng], c[2])
                getattr(block, BLOCK_OF[eng])(body)


def build_and_run(inputs, taps=(), core_ids=None, trace=False, stop_after=None):
    if core_ids is None:
        core_ids = list(range(8))

    nc = bass.Bass("TRN2", target_bir_lowering=False)
    P = Prog()

    dr = {}
    for n, s in [("xin", (T, D)),
                 ("wq", (D, D)), ("bq", (D,)), ("wk", (D, D)), ("bk", (D,)),
                 ("wv", (D, D)), ("bv", (D,)), ("wo", (D, D)), ("bo", (D,)),
                 ("lng", (D,)), ("lnb", (D,)),
                 ("sw1", (1, D, 4096)), ("sb1", (1, 4096)), ("sw2", (1, 4096, D)),
                 ("sb2", (1, D)), ("rw1", (15, D, FF)), ("rb1", (15, FF)),
                 ("rw2", (15, FF, D)), ("rb2", (15, D)), ("gw", (D, 15)), ("gb", (15,))]:
        dr[n] = nc.dram_tensor(n, list(s), F32, kind="ExternalInput").ap()
    idb_d = nc.dram_tensor("idb", [128, 128], BF16, kind="ExternalInput").ap()
    idf_d = nc.dram_tensor("idf", [128, 128], F32, kind="ExternalInput").ap()
    out_d = nc.dram_tensor("xout", [T, D], F32, kind="ExternalOutput").ap()
    g_dram = nc.dram_tensor("gsilu", [T, D], F32, kind="Internal").ap()
    sw1_2d = dr["sw1"].rearrange("o d f -> (o d) f")
    sw2_2d = dr["sw2"].rearrange("o f d -> (o f) d")

    tap_d = {}
    for tname, tshape, tdt in taps:
        tap_d[tname] = nc.dram_tensor("tap_" + tname, list(tshape), tdt,
                                      kind="ExternalOutput").ap()

    def tap(name, src_fn, reads):
        if name in tap_d:
            P.op("sq", lambda s, f=src_fn, n=name: s.dma_start(out=tap_d[n][:], in_=f()),
                 reads=reads, inc=16)

    ctx = ExitStack()
    ctx2 = ExitStack()
    ctxa = ExitStack()
    ctxb = ExitStack()
    sb_ = lambda cx, name, shape, dt, side="left": cx.enter_context(nc.sbuf_tensor(name, shape, dt, side=side))
    ps_ = lambda cx, name, shape: cx.enter_context(nc.psum_tensor(name, shape, F32))

    xnT = sb_(ctx, "xnT", [128, NC8 * T], BF16)
    idb = sb_(ctx, "idbs", [128, 128], BF16)
    idf = sb_(ctx, "idfs", [128, 128], F32)
    onesf = sb_(ctx, "onesf", [65, 128], F32)
    rows_a = sb_(ctx, "rows_a", [1, 1088], F32)
    rows_b = sb_(ctx, "rows_b", [65, 1088], F32)
    smal = sb_(ctx, "smal", [128, 8], F32)
    onesb = sb_(ctx, "onesb", [128, 1], BF16)
    onesbr = sb_(ctx, "onesbr", [65, 128], BF16)
    rows_c = sb_(ctx, "rows_c", [65, 1088], BF16)
    st = sb_(ctx, "st", [128, 24], F32)
    sems = {e: ctx.enter_context(nc.semaphore("sem_" + e)) for e in ENGS}
    Z = smal[:, 1:2]
    EPS = smal[:, 0:1]

    hbuf = sb_(ctx2, "hbuf", [128, NFF * QT], BF16, side="right")
    wA = sb_(ctx2, "wA", [128, NC8 * FF], BF16, side="right")
    wB = sb_(ctx2, "wB", [128, NFF * D], BF16, side="right")
    gw_s = sb_(ctx2, "gw_s", [128, NC8 * 15], F32, side="right")
    rb1T = sb_(ctx2, "rb1T", [128, NFF * NE], F32, side="right")
    gated = sb_(ctx2, "gated", [128, NT * 15], F32, side="right")
    w15 = sb_(ctx2, "w15", [128, 15], F32, side="right")
    wzz = sb_(ctx2, "wzz", [128, 15], F32, side="right")
    wzm = sb_(ctx2, "wzm", [128, 15], F32, side="right")
    mx8 = sb_(ctx2, "mx8", [128, 8], F32, side="right")
    rb2row = sb_(ctx2, "rb2row", [1, D], BF16, side="right")
    pA = ps_(ctx2, "pA", [128, 1024])
    pC = ps_(ctx2, "pC", [128, 1024])
    pD = ps_(ctx2, "pD", [128, 1024])
    p0scr = sb_(ctx2, "p0scr", [128, 5120], BF16, side="right")
    xbuf = [p0scr[:, 0:2048].bitcast(F32), p0scr[:, 0:2048].bitcast(F32)]
    xbf = p0scr[:, 2048:3072]
    sqout = p0scr[:, 2048:3072]
    xnT32 = p0scr[:, 3072:5120].bitcast(F32)

    acc = sb_(ctxa, "acc", [128, NT * D], F32)
    biasrows = sb_(ctxb, "biasrows", [17, FF], BF16, side="right")

    P.op("dv", lambda v: v.memset(onesf[:], 1.0), writes=["onesf"])
    P.op("dv", lambda v: v.memset(smal[:, 0:1], 1e-5), writes=["smal"])
    P.op("dv", lambda v: v.memset(smal[:, 1:2], 0.0), writes=["smal"])
    P.op("dv", lambda v: v.memset(smal[:, 2:3], float(np.log(0.125 * 1.002))), writes=["smal"])
    P.op("dv", lambda v: v.memset(onesb[:], 1.0), writes=["onesb"])
    P.op("dv", lambda v: v.memset(onesbr[:], 1.0), writes=["onesbr"])
    for dst, src, r in [
            (idb[:], idb_d[:], "idb"), (idf[:], idf_d[:], "idf"),
            (rows_a[0:1, 0:D], dr["lnb"][None, :], "rows_a"),
            (rows_a[0:1, D:D + 15], dr["gb"][None, :], "rows_a"),
            (rows_b[0:1, 0:D], dr["bk"][None, :], "rows_b"),
            (rows_b[64:65, 0:D], dr["lng"][None, :], "rows_b"),
            (gw_s[:], dr["gw"].rearrange("(c p) e -> p c e", p=128), "gw_s"),
            ]:
        P.op("sq", lambda s, d=dst, i=src: s.dma_start(out=d, in_=i), writes=[r], inc=16)
    for dst, srcd in [(biasrows[0:15, :], dr["rb1"][:]),
                      (biasrows[15:16, :], dr["sb1"][0:1, 0:FF]),
                      (biasrows[16:17, :], dr["sb1"][0:1, FF:2 * FF])]:
        P.op("gq", lambda g, d=dst, i=srcd: g.dma_start(out=d, in_=i), writes=["brow"], inc=16)

    def _brt(t):
        for c in range(NFF):
            ins = t.matmul(pC[:, c * 17:(c + 1) * 17],
                           lhsT=biasrows[0:17, c * 128:(c + 1) * 128],
                           rhs=idb[0:17, 0:17], start=True, stop=True)
        return ins
    P.op("pe", _brt, reads=["brow", "idb"], writes=["pC0"])
    P.op("dv", lambda v: v.tensor_copy(rb1T[:, 0:NFF * NE], pC[:, 0:NFF * NE]),
         reads=["pC0"], writes=["rb1T"])

    def w1_src(e):
        if e < 15:
            return dr["rw1"][e:e + 1].rearrange("o (c p) f -> p (o c) f", p=128)
        return sw1_2d[:, (e - 15) * FF:(e - 14) * FF].rearrange("(c p) f -> p c f", p=128)

    def w2_src(e):
        if e < 15:
            return dr["rw2"][e:e + 1].rearrange("o (c p) f -> p (o c) f", p=128)
        return sw2_2d[(e - 15) * FF:(e - 14) * FF, :].rearrange("(c p) f -> p c f", p=128)

    P.op("gq", lambda g: g.dma_start(out=rows_c[0:1, 0:D], in_=dr["bq"][None, :]), writes=["rows_c"], inc=16)
    P.op("gq", lambda g: g.dma_start(out=rows_c[32:33, 0:D], in_=dr["bv"][None, :]), writes=["rows_c"], inc=16)
    P.op("gq", lambda g: g.dma_start(out=rows_c[64:65, 0:D], in_=dr["bo"][None, :]), writes=["rows_c"], inc=16)
    P.op("gq", lambda g: g.dma_start(out=wA[:], in_=w1_src(0)), writes=["wA"], inc=16)
    P.op("gq", lambda g: g.dma_start(out=wB[:], in_=w2_src(0)), writes=["wB"], inc=16)

    P.barrier()
    ctxb.close()

    if stop_after == "startup":
        P.final_wait("sq")
        P.replay(nc, sems)
        ctx2.close(); ctxa.close(); ctx.close()
        return _stage_and_run(nc, inputs, core_ids, trace)

    nt_lim = NT if stop_after != "tiles1" else 1

    gsc = [p0scr[:, 0:2048].bitcast(F32), p0scr[:, 2048:4096].bitcast(F32)]

    def emit_rb2row(e):
        if e < 16:
            rsrc = dr["rb2"][e:e + 1, :] if e < 15 else dr["sb2"][0:1, :]
            P.op("gq", lambda g, rsrc=rsrc: g.dma_start(out=rb2row[:], in_=rsrc),
                 writes=["rb2row"], inc=16)

    def emit_wprefetch(e):
        if e + 1 < NE:
            P.op("gq", lambda g, e=e: g.dma_start(out=wA[:], in_=w1_src(e + 1)),
                 writes=["wA"], inc=16)
            P.op("gq", lambda g, e=e: g.dma_start(out=wB[:], in_=w2_src(e + 1)),
                 writes=["wB"], inc=16)

    def emit_eq(e, q):
        if True:
            for f in range(NFF):
                ph = pA[:, 0:512] if f % 2 == 0 else pA[:, 512:1024]
                phr = "pA0" if f % 2 == 0 else "pA1"
                def _m1(t, f=f, q=q, ph=ph):
                    for c in range(NC8):
                        ins = t.matmul(ph, lhsT=wA[:, c * FF + f * 128:c * FF + (f + 1) * 128],
                                       rhs=xnT[:, c * T + q * QT:c * T + (q + 1) * QT],
                                       start=(c == 0), stop=(c == NC8 - 1))
                    return ins
                P.op("pe", _m1, reads=["wA", "xnT"], writes=[phr])
                P.op("ac", lambda s, f=f, e=e, ph=ph: s.activation(
                        hbuf[:, f * QT:(f + 1) * QT], ph, AF.Gelu,
                        bias=rb1T[:, f * NE + e:f * NE + e + 1], scale=1.0),
                     reads=[phr, "rb1T"], writes=[f"hb{f // 4}"])
            for j2 in range(4):
                J = q * 4 + j2
                pm = pC if J % 2 == 0 else pD
                pmr = None
                def _m2(t, j2=j2, pm=pm, e=e):
                    for cf in range(NFF):
                        for s_ in range(2):
                            ins = t.matmul(pm[:, s_ * 512:(s_ + 1) * 512],
                                           lhsT=hbuf[:, cf * QT + j2 * 128:cf * QT + (j2 + 1) * 128],
                                           rhs=wB[:, cf * D + s_ * 512:cf * D + (s_ + 1) * 512],
                                           start=(cf == 0), stop=(cf == NFF - 1 and e >= 16))
                    if e < 16:
                        for s_ in range(2):
                            ins = t.matmul(pm[:, s_ * 512:(s_ + 1) * 512], lhsT=onesbr[0:1, :],
                                           rhs=rb2row[0:1, s_ * 512:(s_ + 1) * 512],
                                           start=False, stop=True)
                    return ins
                pmrs = ["pC0", "pC1"] if J % 2 == 0 else ["pD0", "pD1"]
                mm2_reads = ["hb0", "hb1", "hb2", "hb3", "wB", "onesbr"]
                if e < 16:
                    mm2_reads.append("rb2row")
                P.op("pe", _m2, reads=mm2_reads, writes=pmrs)
                a_J = acc[:, J * D:(J + 1) * D]
                if e < 15:
                    P.op("dv", lambda v, a=a_J, J=J, e=e, pm=pm: v.scalar_tensor_tensor(
                            a, pm[:], gated[:, J * 15 + e:J * 15 + e + 1], a,
                            op0=ALU.mult, op1=ALU.add),
                         reads=pmrs + ["gated", f"acc{J % 4}"], writes=[f"acc{J % 4}"])
                else:
                    P.op("dv", lambda v, a=a_J, pm=pm: v.tensor_tensor(a, a, pm[:], op=ALU.add),
                         reads=pmrs + [f"acc{J % 4}"], writes=[f"acc{J % 4}"])
                if e == NE - 1:
                    gv = gsc[J % 2]
                    gr = f"p0g{J % 2}"
                    P.op("ac", lambda s, J=J, gv=gv: s.activation(gv, acc[:, J * D:(J + 1) * D],
                                                                  AF.Silu, bias=Z, scale=1.0),
                         reads=[f"acc{J % 4}", "smal"],
                         writes=[gr, "xb", "xbfr", "xnT32"])
                    P.op("gq", lambda g, J=J, gv=gv: g.dma_start(
                            out=g_dram[J * 128:(J + 1) * 128, :], in_=gv),
                         reads=[gr], writes=[f"gd{J}"], inc=16)



    for j in range(nt_lim):
        xb = xbuf[0]
        xr = "xb"
        a_j = acc[:, j * D:(j + 1) * D]
        ar = f"acc{j % 4}"
        sb0 = 12 * (j % 2)
        sr = f"st{j % 2}"
        pTR = pA if j % 2 == 0 else pD
        pTRr = ["pA0", "pA1"] if j % 2 == 0 else ["pD0", "pD1"]
        pXR = pD if j % 2 == 0 else pA
        pXRr = ["pD0", "pD1"] if j % 2 == 0 else ["pA0", "pA1"]
        pG = pC[:, 0:15] if j % 2 == 0 else pC[:, 512:527]
        pGr = "pC0" if j % 2 == 0 else "pC1"
        P.op("sq", lambda s, xb=xb, j=j: s.dma_start(out=xb, in_=dr["xin"][j * 128:(j + 1) * 128, :]),
             writes=[xr], inc=16)
        P.op("dv", lambda v, xb=xb, sb0=sb0: v.reduce_sum(st[:, sb0:sb0 + 1], xb, axis=AX.X),
             reads=[xr], writes=[sr])
        P.op("ac", lambda s, xb=xb, sb0=sb0: s.activation(sqout, xb, AF.Square, bias=Z, scale=1.0,
                                                 accum_out=st[:, sb0 + 2:sb0 + 3]),
             reads=[xr, "smal"], writes=["xbfr", sr])
        P.op("dv", lambda v, sb0=sb0: v.tensor_scalar_mul(st[:, sb0 + 1:sb0 + 2], st[:, sb0:sb0 + 1], 1.0 / D),
             reads=[sr], writes=[sr])
        P.op("dv", lambda v, sb0=sb0: v.tensor_tensor(st[:, sb0 + 5:sb0 + 6], st[:, sb0 + 1:sb0 + 2],
                                                      st[:, sb0 + 1:sb0 + 2], op=ALU.mult),
             reads=[sr], writes=[sr])
        P.op("dv", lambda v, sb0=sb0: v.scalar_tensor_tensor(st[:, sb0 + 6:sb0 + 7], st[:, sb0 + 2:sb0 + 3],
                                                             1.0 / D, st[:, sb0 + 5:sb0 + 6],
                                                             op0=ALU.mult, op1=ALU.subtract),
             reads=[sr], writes=[sr])
        P.op("ac", lambda s, sb0=sb0: s.activation(st[:, sb0 + 3:sb0 + 4], st[:, sb0 + 6:sb0 + 7], AF.Ln, bias=EPS, scale=1.0),
             reads=[sr, "smal"], writes=[sr])
        P.op("ac", lambda s, sb0=sb0: s.activation(st[:, sb0 + 4:sb0 + 5], st[:, sb0 + 3:sb0 + 4], AF.Exp, bias=Z, scale=-0.5),
             reads=[sr, "smal"], writes=[sr])
        P.op("dv", lambda v, sb0=sb0: v.tensor_tensor(st[:, sb0 + 7:sb0 + 8], st[:, sb0 + 1:sb0 + 2],
                                                      st[:, sb0 + 4:sb0 + 5], op=ALU.mult),
             reads=[sr], writes=[sr])
        P.op("dv", lambda v, sb0=sb0: v.tensor_scalar_mul(st[:, sb0 + 9:sb0 + 10], st[:, sb0 + 7:sb0 + 8], -1.0),
             reads=[sr], writes=[sr])
        P.op("ac", lambda s, xb=xb, a=a_j, sb0=sb0: s.activation(a, xb, AF.Identity,
                                                bias=st[:, sb0 + 9:sb0 + 10], scale=st[:, sb0 + 4:sb0 + 5]),
             reads=[xr, sr], writes=[ar])
        P.op("ac", lambda s, a=a_j: s.activation(xbf, a, AF.Identity, bias=Z, scale=1.0),
             reads=[ar, "smal"], writes=["xbfr"])
        def _tr(t, pTR=pTR):
            for c in range(NC8):
                ins = t.matmul(pTR[:, c * 128:(c + 1) * 128], lhsT=xbf[:, c * 128:(c + 1) * 128],
                               rhs=idb[:], start=True, stop=True)
            return ins
        P.op("pe", _tr, reads=["xbfr", "idb"], writes=pTRr)
        xnT_view = xnT.ap().rearrange("p (c t) -> p c t", c=NC8)[:, :, j * 128:(j + 1) * 128]
        P.op("ac", lambda s, v_=xnT_view, pTR=pTR: s.activation(v_, pTR[:], AF.Identity, bias=Z, scale=1.0),
             reads=pTRr + ["smal"], writes=["xnT"])

        def _xtr32(t, j=j, pXR=pXR):
            for c in range(NC8):
                ins = t.matmul(pXR[:, c * 128:(c + 1) * 128],
                               lhsT=acc[:, j * D + c * 128:j * D + (c + 1) * 128],
                               rhs=idf[:], start=True, stop=True)
            return ins
        P.op("pe", _xtr32, reads=[ar, "idf"], writes=pXRr)
        P.op("dv", lambda v, pXR=pXR: v.tensor_copy(xnT32[:], pXR[:]), reads=pXRr, writes=["xnT32"])
        def _gmm(t, pG=pG):
            for c in range(NC8):
                t.matmul(pG, lhsT=xnT32[:, c * 128:(c + 1) * 128],
                         rhs=gw_s[:, c * 15:(c + 1) * 15], start=(c == 0), stop=False)
            return t.matmul(pG, lhsT=onesf[0:1, :], rhs=rows_a[0:1, D:D + 15],
                            start=False, stop=True)
        P.op("pe", _gmm, reads=["xnT32", "gw_s", "rows_a", "onesf"], writes=[pGr])
        P.op("dv", lambda v, pG=pG, sb0=sb0: v.reduce_max(st[:, sb0 + 5:sb0 + 6], pG, axis=AX.X),
             reads=[pGr], writes=[sr])
        P.op("dv", lambda v, sb0=sb0: v.tensor_scalar_mul(st[:, sb0 + 6:sb0 + 7], st[:, sb0 + 5:sb0 + 6], -1.0),
             reads=[sr], writes=[sr])
        P.op("ac", lambda s, pG=pG, sb0=sb0: s.activation(w15[:], pG, AF.Exp, bias=st[:, sb0 + 6:sb0 + 7], scale=1.0,
                                          accum_out=st[:, sb0 + 7:sb0 + 8]),
             reads=[pGr, sr], writes=["w15", sr])
        P.op("dv", lambda v, sb0=sb0: v.reciprocal(st[:, sb0 + 8:sb0 + 9], st[:, sb0 + 7:sb0 + 8]),
             reads=[sr], writes=[sr])
        P.op("dv", lambda v, sb0=sb0: v.tensor_scalar(w15[:], w15[:], st[:, sb0 + 8:sb0 + 9], None, op0=ALU.mult),
             reads=["w15", sr], writes=["w15"])
        P.op("dv", lambda v: v.max(out=mx8[:], in_=w15[:]), reads=["w15"], writes=["mx8"])
        P.op("dv", lambda v: v.memset(mx8[:, 3:8], -1e30), writes=["mx8"])
        P.op("dv", lambda v: v.match_replace(out=wzz[:], in_to_replace=mx8[:], in_values=w15[:],
                                             imm_value=-1e30),
             reads=["mx8", "w15"], writes=["wzz", "mx8"])
        P.op("dv", lambda v: v.tensor_tensor(wzm[:], w15[:], wzz[:], op=ALU.not_equal),
             reads=["w15", "wzz"], writes=["wzm"])
        P.op("dv", lambda v, j=j: v.tensor_tensor(gated[:, j * 15:(j + 1) * 15], wzm[:], w15[:],
                                                  op=ALU.mult),
             reads=["wzm", "w15"], writes=["gated"])

        if j % 4 == 3 and stop_after != "p1":
            if j == 3:
                emit_rb2row(0)
            emit_eq(0, j // 4)
            if j == 15:
                emit_wprefetch(0)

    tap("xnT", lambda: xnT[:], ["xnT"])
    tap("acc1", lambda: acc[:], ["acc0", "acc1", "acc2", "acc3"])
    tap("gated", lambda: gated[:], ["gated"])
    if stop_after == "tiles1":
        P.final_wait("sq")
        P.replay(nc, sems)
        ctx2.close(); ctxa.close(); ctx.close()
        return _stage_and_run(nc, inputs, core_ids, trace)

    for e in (range(1, NE) if stop_after != "p1" else []):
        emit_rb2row(e)
        for q in range(NQ):
            emit_eq(e, q)
        emit_wprefetch(e)

    tap("accf", lambda: acc[:], ["acc0", "acc1", "acc2", "acc3"])

    if stop_after in ("p1", "p2"):
        P.final_wait("sq")
        P.replay(nc, sems)
        ctxb.close(); ctx2.close(); ctxa.close(); ctx.close()
        return _stage_and_run(nc, inputs, core_ids, trace)

    P.barrier()
    ctx2.close()

    ctx3 = ExitStack()
    khT = sb_(ctx3, "khT", [128, NC8 * T], BF16, side="right")
    vtm = sb_(ctx3, "vtm", [128, NT * 1040], BF16, side="right")
    wq_s = sb_(ctx3, "wq_s", [128, NC8 * D], BF16, side="right")
    wo_s = sb_(ctx3, "wo_s", [128, NC8 * D], BF16, side="right")
    bkT = sb_(ctx3, "bkT", [128, 8], F32, side="right")
    psS = ctx3.enter_context(nc.psum_tensor("psS", [128, 2048], F32))
    psQ = ctx3.enter_context(nc.psum_tensor("psQ", [128, 1024], F32))
    psT = ctx3.enter_context(nc.psum_tensor("psT", [128, 1024], F32))
    sunit = [psS[:, 0:1024], psS[:, 1024:2048], psT[:, 0:1024]]

    P.op("gq", lambda g: g.dma_start(out=wq_s[:], in_=dr["wk"].rearrange("(c p) f -> p c f", p=128)),
         writes=["wq_s"], inc=16)
    P.op("gq", lambda g: g.dma_start(out=wo_s[:], in_=dr["wv"].rearrange("(c p) f -> p c f", p=128)),
         writes=["wo_s"], inc=16)

    def _bkt(t):
        for c in range(NC8):
            ins = t.matmul(psT[:, c:c + 1], lhsT=rows_b[0:1, c * 128:(c + 1) * 128],
                           rhs=onesf[0:1, 0:1], start=True, stop=True)
        return ins
    P.op("pe", _bkt, reads=["rows_b", "onesf"], writes=["psU2"])
    P.op("dv", lambda v: v.tensor_copy(bkT[:], psT[:, 0:8]), reads=["psU2"], writes=["bkT"])

    for cof in range(NC8):
        for hf in range(2):
            flip = (cof * 2 + hf) % 2 == 0
            pk = psQ if flip else psT
            pkw = ["psQa", "psQb"] if flip else ["psU2"]
            def _kmm(t, cof=cof, hf=hf, pk=pk):
                for c in range(NC8):
                    for s_ in range(2):
                        ins = t.matmul(
                            pk[:, s_ * 512:(s_ + 1) * 512],
                            lhsT=wq_s[:, c * D + cof * 128:c * D + (cof + 1) * 128],
                            rhs=xnT[:, c * T + hf * 1024 + s_ * 512:c * T + hf * 1024 + (s_ + 1) * 512],
                            start=(c == 0), stop=(c == NC8 - 1))
                return ins
            P.op("pe", _kmm, reads=["wq_s", "xnT"], writes=pkw)
            P.op("ac", lambda s, cof=cof, hf=hf, pk=pk: s.activation(
                    khT[:, cof * T + hf * 1024:cof * T + (hf + 1) * 1024], pk[:], AF.Identity,
                    bias=bkT[:, cof:cof + 1], scale=1.0),
                 reads=pkw + ["bkT"], writes=["khT"])
    P.op("dv", lambda v: v.memset(vtm[:], 1.0), writes=["vtm"])
    for kt in range(NT):
        flip = kt % 2 == 0
        pk = psQ if flip else psT
        pkw = ["psQa", "psQb"] if flip else ["psU2"]
        def _vmm(t, kt=kt, pk=pk):
            for s_ in range(2):
                for c in range(NC8):
                    t.matmul(pk[:, s_ * 512:(s_ + 1) * 512],
                             lhsT=xnT[:, c * T + kt * 128:c * T + (kt + 1) * 128],
                             rhs=wo_s[:, c * D + s_ * 512:c * D + (s_ + 1) * 512],
                             start=(c == 0), stop=False)
                ins = t.matmul(pk[:, s_ * 512:(s_ + 1) * 512], lhsT=onesbr[32:33, :],
                               rhs=rows_c[32:33, s_ * 512:(s_ + 1) * 512], start=False, stop=True)
            return ins
        P.op("pe", _vmm, reads=["xnT", "wo_s", "rows_c", "onesbr"], writes=pkw)
        vv = vtm.ap().rearrange("p (kt h e) -> p kt h e", kt=NT, h=H)[:, kt, :, 0:64]
        P.op("ac", lambda s, vv=vv, pk=pk: s.activation(vv, pk[:], AF.Identity, bias=Z, scale=1.0),
             reads=pkw + ["smal"], writes=["vtm"])

    P.op("gq", lambda g: g.dma_start(out=wq_s[:], in_=dr["wq"].rearrange("(c p) f -> p c f", p=128)),
         writes=["wq_s"], inc=16)
    P.op("gq", lambda g: g.dma_start(out=wo_s[:], in_=dr["wo"].rearrange("(c p) f -> p c f", p=128)),
         writes=["wo_s"], inc=16)

    tap("khT", lambda: khT[:], ["khT"])
    tap("vtm", lambda: vtm[:], ["vtm"])

    if stop_after == "p3":
        P.final_wait("sq")
        P.replay(nc, sems)
        ctxb.close(); ctx3.close(); ctxa.close(); ctx.close()
        return _stage_and_run(nc, inputs, core_ids, trace)

    P.barrier()
    ctxa.close()

    ctx4 = ExitStack()
    qnP = [sb_(ctx4, "qn0t", [128, D], F32), sb_(ctx4, "qn1t", [128, D], F32)]
    qnbP = [sb_(ctx4, "qnb0", [128, D], BF16), sb_(ctx4, "qnb1", [128, D], BF16)]
    qTbP = [sb_(ctx4, "qTb0", [128, D], BF16), sb_(ctx4, "qTb1", [128, D], BF16)]
    expST = sb_(ctx4, "expST", [128, 4096], BF16)
    snk = sb_(ctx4, "snk", [128, D], BF16)
    ao = sb_(ctx4, "ao", [128, D], F32)
    abf = sb_(ctx4, "abf", [128, D], BF16)
    aTb = sb_(ctx4, "aTb", [128, D], BF16)
    xg = sb_(ctx4, "xg", [128, 2 * D], F32)
    ob = sb_(ctx4, "ob", [128, D], F32)
    gbc = sb_(ctx4, "gbc", [128, D], F32)
    bbc = sb_(ctx4, "bbc", [128, D], F32)
    mh = sb_(ctx4, "mh", [128, 16], F32)
    qst = sb_(ctx4, "qst", [128, 16], F32)

    def _gbcmm(t):
        for s_ in range(2):
            ins = t.matmul(psQ[:, s_ * 512:(s_ + 1) * 512], lhsT=onesf[64:65, :],
                           rhs=rows_b[64:65, s_ * 512:(s_ + 1) * 512], start=True, stop=True)
        return ins
    P.op("pe", _gbcmm, reads=["rows_b", "onesf"], writes=["psQa", "psQb"])
    P.op("dv", lambda v: v.tensor_copy(gbc[:], psQ[:]), reads=["psQa", "psQb"], writes=["gbc"])
    def _bbcmm(t):
        for s_ in range(2):
            ins = t.matmul(psQ[:, s_ * 512:(s_ + 1) * 512], lhsT=onesf[0:1, :],
                           rhs=rows_a[0:1, s_ * 512:(s_ + 1) * 512], start=True, stop=True)
        return ins
    P.op("pe", _bbcmm, reads=["rows_a", "onesf"], writes=["psQa", "psQb"])
    P.op("dv", lambda v: v.tensor_copy(bbc[:], psQ[:]), reads=["psQa", "psQb"], writes=["bbc"])

    unext = [0]
    def take_unit():
        u = unext[0] % 3
        unext[0] += 1
        return sunit[u], f"psU{u}"

    def emit_qmm_ln(jj):
        pp = jj % 2
        qn_, qnb_ = qnP[pp], qnbP[pp]
        rq, rqb = f"qn{pp}", f"qnb{pp}"
        sq0 = 8 * pp
        rst = f"qst{pp}"
        pS, rS = take_unit()
        def _qmm(t, jj=jj, pS=pS):
            for s_ in range(2):
                for c in range(NC8):
                    t.matmul(pS[:, s_ * 512:(s_ + 1) * 512],
                             lhsT=xnT[:, c * T + jj * 128:c * T + (jj + 1) * 128],
                             rhs=wq_s[:, c * D + s_ * 512:c * D + (s_ + 1) * 512],
                             start=(c == 0), stop=False)
                ins = t.matmul(pS[:, s_ * 512:(s_ + 1) * 512], lhsT=onesbr[0:1, :],
                               rhs=rows_c[0:1, s_ * 512:(s_ + 1) * 512], start=False, stop=True)
            return ins
        P.op("pe", _qmm, reads=["xnT", "wq_s", "rows_c", "onesbr"], writes=[rS])
        P.op("dv", lambda v, pS=pS, sq0=sq0: v.reduce_sum(qst[:, sq0:sq0 + 1], pS[:], axis=AX.X),
             reads=[rS], writes=[rst])
        P.op("ac", lambda s, pS=pS, sq0=sq0: s.activation(snk[:], pS[:], AF.Square, bias=Z, scale=1.0,
                                                          accum_out=qst[:, sq0 + 1:sq0 + 2]),
             reads=[rS, "smal"], writes=["snk", rst])
        P.op("dv", lambda v, sq0=sq0: v.tensor_scalar_mul(qst[:, sq0 + 2:sq0 + 3], qst[:, sq0:sq0 + 1], 1.0 / D),
             reads=[rst], writes=[rst])
        P.op("dv", lambda v, sq0=sq0: v.tensor_tensor(qst[:, sq0 + 3:sq0 + 4], qst[:, sq0 + 2:sq0 + 3],
                                                      qst[:, sq0 + 2:sq0 + 3], op=ALU.mult),
             reads=[rst], writes=[rst])
        P.op("dv", lambda v, sq0=sq0: v.scalar_tensor_tensor(qst[:, sq0 + 4:sq0 + 5], qst[:, sq0 + 1:sq0 + 2],
                                                             1.0 / D, qst[:, sq0 + 3:sq0 + 4],
                                                             op0=ALU.mult, op1=ALU.subtract),
             reads=[rst], writes=[rst])
        P.op("ac", lambda s, sq0=sq0: s.activation(qst[:, sq0 + 5:sq0 + 6], qst[:, sq0 + 4:sq0 + 5],
                                                   AF.Ln, bias=EPS, scale=1.0),
             reads=[rst, "smal"], writes=[rst])
        P.op("ac", lambda s, sq0=sq0: s.activation(qst[:, sq0 + 6:sq0 + 7], qst[:, sq0 + 5:sq0 + 6],
                                                   AF.Exp, bias=Z, scale=-0.5),
             reads=[rst, "smal"], writes=[rst])
        P.op("dv", lambda v, sq0=sq0: v.tensor_tensor(qst[:, sq0 + 7:sq0 + 8], qst[:, sq0 + 2:sq0 + 3],
                                                      qst[:, sq0 + 6:sq0 + 7], op=ALU.mult),
             reads=[rst], writes=[rst])
        P.op("dv", lambda v, sq0=sq0: v.tensor_scalar_mul(qst[:, sq0 + 7:sq0 + 8], qst[:, sq0 + 7:sq0 + 8], -1.0),
             reads=[rst], writes=[rst])
        P.op("ac", lambda s, qn_=qn_, pS=pS, sq0=sq0: s.activation(
                qn_[:], pS[:], AF.Identity, bias=qst[:, sq0 + 7:sq0 + 8], scale=qst[:, sq0 + 6:sq0 + 7]),
             reads=[rS, rst], writes=[rq])
        P.op("dv", lambda v, qn_=qn_: v.tensor_tensor(qn_[:], qn_[:], gbc[:], op=ALU.mult),
             reads=[rq, "gbc"], writes=[rq])
        P.op("dv", lambda v, qn_=qn_: v.tensor_tensor(qn_[:], qn_[:], bbc[:], op=ALU.add),
             reads=[rq, "bbc"], writes=[rq])
        P.op("dv", lambda v, qn_=qn_, qnb_=qnb_: v.tensor_copy(qnb_[:], qn_[:]), reads=[rq], writes=[rqb])

    def emit_qtr(jj):
        pp = jj % 2
        qnb_, qTb_ = qnbP[pp], qTbP[pp]
        rqb, rqt = f"qnb{pp}", f"qTb{pp}"
        pS, rS = take_unit()
        def _qtr(t, pS=pS, qnb_=qnb_):
            for c in range(NC8):
                ins = t.matmul(pS[:, c * 128:(c + 1) * 128], lhsT=qnb_[:, c * 128:(c + 1) * 128],
                               rhs=idb[:], start=True, stop=True)
            return ins
        P.op("pe", _qtr, reads=[rqb, "idb"], writes=[rS])
        P.op("dv", lambda v, pS=pS, qTb_=qTb_: v.tensor_copy(qTb_[:], pS[:]), reads=[rS], writes=[rqt])

    def emit_av(h):
        par = h % 2
        avs = psQ[:, 0:65] if par == 0 else psQ[:, 512:577]
        avr = "psQa" if par == 0 else "psQb"
        def _av(t, h=h, avs=avs, par=par):
            for kt in range(NT):
                ins = t.matmul(avs, lhsT=expST[:, par * 2048 + kt * 128:par * 2048 + (kt + 1) * 128],
                               rhs=vtm[:, kt * 1040 + h * 65:kt * 1040 + (h + 1) * 65],
                               start=(kt == 0), stop=(kt == NT - 1))
            return ins
        P.op("pe", _av, reads=["expST" + ("a" if par == 0 else "b"), "vtm"], writes=[avr])
        mhr = f"mh{par}"
        P.op("dv", lambda v, h=h, avs=avs: v.tensor_scalar(mh[:, 4 + (h % 2):5 + (h % 2)],
                                                           avs[:, 64:65], 1e-8, None, op0=ALU.add),
             reads=[avr], writes=[mhr])
        P.op("dv", lambda v, h=h: v.reciprocal(mh[:, 8 + (h % 2):9 + (h % 2)],
                                               mh[:, 4 + (h % 2):5 + (h % 2)]),
             reads=[mhr], writes=[mhr])
        P.op("dv", lambda v, h=h, avs=avs: v.tensor_scalar(
                ao[:, h * 64:(h + 1) * 64], avs[:, 0:64],
                mh[:, 8 + (h % 2):9 + (h % 2)], None, op0=ALU.mult),
             reads=[avr, mhr], writes=["ao"])

    emit_qmm_ln(0)
    emit_qtr(0)
    for j in range(NT):
        pp = j % 2
        qn_, qTb_ = qnP[pp], qTbP[pp]
        rq, rqt = f"qn{pp}", f"qTb{pp}"
        xv = xg[:, 0:D]
        gv4 = xg[:, D:2 * D]
        xrr, grr = "x4", "g4"
        P.op("sq", lambda s, j=j, xv=xv: s.dma_start(out=xv, in_=dr["xin"][j * 128:(j + 1) * 128, :]),
             writes=[xrr], inc=16)
        P.op("sq", lambda s, j=j, gv=gv4: s.dma_start(out=gv, in_=g_dram[j * 128:(j + 1) * 128, :]),
             reads=[f"gd{j}"], writes=[grr], inc=16)
        for h in range(H):
            poff = 64 * (h % 2)
            coff = (h // 2) * 128
            par = h % 2
            for kh in range(2):
                pS, rS = take_unit()
                def _stmm(t, poff=poff, coff=coff, h=h, kh=kh, pS=pS, qTb_=qTb_):
                    for kt in range(8):
                        ktg = kh * 8 + kt
                        ins = t.matmul(pS[:, kt * 128:(kt + 1) * 128],
                                       lhsT=khT[poff:poff + 64, (h // 2) * T + ktg * 128:(h // 2) * T + (ktg + 1) * 128],
                                       rhs=qTb_[poff:poff + 64, coff:coff + 128],
                                       start=True, stop=True)
                    return ins
                P.op("pe", _stmm, reads=[rqt, "khT"], writes=[rS])
                P.op("ac", lambda s, par=par, kh=kh, pS=pS: s.activation(
                        expST[:, par * 2048 + kh * 1024:par * 2048 + (kh + 1) * 1024], pS[:],
                        AF.Exp, bias=Z, scale=0.125),
                     reads=[rS, "smal"], writes=["expST" + ("a" if par == 0 else "b")])
            if h > 0:
                emit_av(h - 1)
            if h == 7 and j + 1 < NT:
                emit_qmm_ln(j + 1)
            if h == 11 and j + 1 < NT:
                emit_qtr(j + 1)
        emit_av(H - 1)
        P.op("dv", lambda v, qn_=qn_: v.tensor_tensor(abf[:], ao[:], qn_[:], op=ALU.add),
             reads=["ao", rq], writes=["abf"])
        pS, rS = take_unit()
        def _atr(t, pS=pS):
            for c in range(NC8):
                ins = t.matmul(pS[:, c * 128:(c + 1) * 128], lhsT=abf[:, c * 128:(c + 1) * 128],
                               rhs=idb[:], start=True, stop=True)
            return ins
        P.op("pe", _atr, reads=["abf", "idb"], writes=[rS])
        P.op("dv", lambda v, pS=pS: v.tensor_copy(aTb[:], pS[:]), reads=[rS], writes=["aTb"])
        def _omm(t):
            for s_ in range(2):
                for c in range(NC8):
                    t.matmul(psQ[:, s_ * 512:(s_ + 1) * 512], lhsT=aTb[:, c * 128:(c + 1) * 128],
                             rhs=wo_s[:, c * D + s_ * 512:c * D + (s_ + 1) * 512],
                             start=(c == 0), stop=False)
                ins = t.matmul(psQ[:, s_ * 512:(s_ + 1) * 512], lhsT=onesbr[64:65, :],
                               rhs=rows_c[64:65, s_ * 512:(s_ + 1) * 512], start=False, stop=True)
            return ins
        P.op("pe", _omm, reads=["aTb", "wo_s", "rows_c", "onesbr"], writes=["psQa", "psQb"])
        P.op("dv", lambda v, gv=gv4: v.tensor_tensor(ob[:], psQ[:], gv, op=ALU.mult),
             reads=["psQa", "psQb", grr], writes=["obr"])
        P.op("dv", lambda v, xv=xv: v.tensor_tensor(ob[:], ob[:], xv, op=ALU.add),
             reads=["obr", xrr], writes=["obr"])
        P.op("sq", lambda s, j=j: s.dma_start(out=out_d[j * 128:(j + 1) * 128, :], in_=ob[:]),
             reads=["obr"], inc=16)
        if j == 0:
            tap("qn0", lambda: qnP[0][:], ["qn0"])
            tap("ao0", lambda: ao[:], ["ao"])

    P.final_wait("sq")
    P.replay(nc, sems)
    ctx4.close()
    ctx3.close()
    ctx.close()
    return _stage_and_run(nc, inputs, core_ids, trace)


def _stage_and_run(nc, inputs, core_ids, trace):
    x_full = np.ascontiguousarray(np.asarray(inputs["L0_attn_x"], dtype=np.float32))
    base = {
        "wq": inputs["L0_attn_Wq"], "bq": inputs["L0_attn_bq"],
        "wk": inputs["L0_attn_Wk"], "bk": inputs["L0_attn_bk"],
        "wv": inputs["L0_attn_Wv"], "bv": inputs["L0_attn_bv"],
        "wo": inputs["L0_attn_Wo"], "bo": inputs["L0_attn_bo"],
        "lng": inputs["L0_attn_ln_g"], "lnb": inputs["L0_attn_ln_b"],
        "sw1": inputs["L0_ffn_sW1"], "sb1": inputs["L0_ffn_sb1"],
        "sw2": inputs["L0_ffn_sW2"], "sb2": inputs["L0_ffn_sb2"],
        "rw1": inputs["L0_ffn_rW1"], "rb1": inputs["L0_ffn_rb1"],
        "rw2": inputs["L0_ffn_rW2"], "rb2": inputs["L0_ffn_rb2"],
        "gw": inputs["L0_ffn_gW"], "gb": inputs["L0_ffn_gb"],
    }
    base = {k: np.ascontiguousarray(np.asarray(v, dtype=np.float32)) for k, v in base.items()}
    base["idb"] = np.eye(128, dtype=BF)
    base["idf"] = np.eye(128, dtype=np.float32)
    in_maps = []
    for c in range(len(core_ids)):
        m = dict(base)
        m["xin"] = np.ascontiguousarray(x_full[c])
        in_maps.append(m)
    return run_bass_kernel_spmd(nc, in_maps, core_ids=core_ids, trace=trace)


def kernel(**inputs):
    res = build_and_run(inputs)
    return np.stack([r["xout"] for r in res.results], axis=0).astype(np.float32)


# version 22
# speedup vs baseline: 1.0055x; 1.0055x over previous
import numpy as np
import ml_dtypes
from contextlib import ExitStack

import concourse.bass as bass
import concourse.mybir as mybir
from concourse.bass_utils import run_bass_kernel_spmd

F32 = mybir.dt.float32
BF16 = mybir.dt.bfloat16
AF = mybir.ActivationFunctionType
ALU = mybir.AluOpType
AX = mybir.AxisListType
BF = ml_dtypes.bfloat16

T = 2048
D = 1024
NT = 16
NC8 = 8
H = 16
FF = 2048
NFF = 16
NE = 17
QT = 512
NQ = 4

ENGS = ("pe", "dv", "ac", "gq", "sq")
BLOCK_OF = {"pe": "tensor", "dv": "vector", "ac": "scalar", "gq": "gpsimd", "sq": "sync"}


class Prog:

    def __init__(self):
        self.cmds = {e: [] for e in ENGS}
        self.cnt = dict.fromkeys(ENGS, 0)
        self.hwm = {e: {} for e in ENGS}
        self.res = {}

    def _wait(self, eng, tok):
        p, v = tok
        if p == eng == "pe":
            return
        if v <= self.hwm[eng].get(p, 0):
            return
        self.hwm[eng][p] = v
        self.cmds[eng].append(("w", p, v))

    def op(self, eng, fn, reads=(), writes=(), inc=1):
        toks = []
        for r in reads:
            e = self.res.get(r)
            if e is not None and e[0] is not None:
                toks.append(e[0])
        for w in writes:
            e = self.res.get(w)
            if e is not None:
                if e[0] is not None:
                    toks.append(e[0])
                toks.extend(e[1])
        for t in sorted(set(toks)):
            self._wait(eng, t)
        self.cnt[eng] += inc
        tok = (eng, self.cnt[eng])
        self.cmds[eng].append(("e", fn, inc))
        if inc == 16:
            self.hwm[eng][eng] = self.cnt[eng]
            self.cmds[eng].append(("w", eng, self.cnt[eng]))
        for r in reads:
            self.res.setdefault(r, [None, []])[1].append(tok)
        for w in writes:
            self.res[w] = [tok, []]
        return tok

    def barrier(self):
        toks = [(e, self.cnt[e]) for e in ENGS if self.cnt[e] > 0]
        for e in ENGS:
            for t in toks:
                if t[0] != e:
                    self._wait(e, t)

    def final_wait(self, eng):
        self.cmds[eng].append(("w", eng, self.cnt[eng]))

    def replay(self, nc, sems):
        with nc.Block() as block:
            for eng in ENGS:
                def body(E, eng=eng):
                    for c in self.cmds[eng]:
                        if c[0] == "w":
                            E.wait_ge(sems[c[1]], c[2])
                        else:
                            c[1](E).then_inc(sems[eng], c[2])
                getattr(block, BLOCK_OF[eng])(body)


def build_and_run(inputs, taps=(), core_ids=None, trace=False, stop_after=None):
    if core_ids is None:
        core_ids = list(range(8))

    nc = bass.Bass("TRN2", target_bir_lowering=False)
    P = Prog()

    dr = {}
    for n, s in [("xin", (T, D)),
                 ("wq", (D, D)), ("bq", (D,)), ("wk", (D, D)), ("bk", (D,)),
                 ("wv", (D, D)), ("bv", (D,)), ("wo", (D, D)), ("bo", (D,)),
                 ("lng", (D,)), ("lnb", (D,)),
                 ("sw1", (1, D, 4096)), ("sb1", (1, 4096)), ("sw2", (1, 4096, D)),
                 ("sb2", (1, D)), ("rw1", (15, D, FF)), ("rb1", (15, FF)),
                 ("rw2", (15, FF, D)), ("rb2", (15, D)), ("gw", (D, 15)), ("gb", (15,))]:
        dr[n] = nc.dram_tensor(n, list(s), F32, kind="ExternalInput").ap()
    idb_d = nc.dram_tensor("idb", [128, 128], BF16, kind="ExternalInput").ap()
    idf_d = nc.dram_tensor("idf", [128, 128], F32, kind="ExternalInput").ap()
    out_d = nc.dram_tensor("xout", [T, D], F32, kind="ExternalOutput").ap()
    g_dram = nc.dram_tensor("gsilu", [T, D], F32, kind="Internal").ap()
    sw1_2d = dr["sw1"].rearrange("o d f -> (o d) f")
    sw2_2d = dr["sw2"].rearrange("o f d -> (o f) d")

    tap_d = {}
    for tname, tshape, tdt in taps:
        tap_d[tname] = nc.dram_tensor("tap_" + tname, list(tshape), tdt,
                                      kind="ExternalOutput").ap()

    def tap(name, src_fn, reads):
        if name in tap_d:
            P.op("sq", lambda s, f=src_fn, n=name: s.dma_start(out=tap_d[n][:], in_=f()),
                 reads=reads, inc=16)

    ctx = ExitStack()
    ctx2 = ExitStack()
    ctxa = ExitStack()
    ctxb = ExitStack()
    sb_ = lambda cx, name, shape, dt, side="left": cx.enter_context(nc.sbuf_tensor(name, shape, dt, side=side))
    ps_ = lambda cx, name, shape: cx.enter_context(nc.psum_tensor(name, shape, F32))

    xnT = sb_(ctx, "xnT", [128, NC8 * T], BF16)
    idb = sb_(ctx, "idbs", [128, 128], BF16)
    idf = sb_(ctx, "idfs", [128, 128], F32)
    onesf = sb_(ctx, "onesf", [65, 128], F32)
    rows_a = sb_(ctx, "rows_a", [1, 1088], F32)
    rows_b = sb_(ctx, "rows_b", [65, 1088], F32)
    smal = sb_(ctx, "smal", [128, 8], F32)
    onesb = sb_(ctx, "onesb", [128, 1], BF16)
    onesbr = sb_(ctx, "onesbr", [65, 128], BF16)
    rows_c = sb_(ctx, "rows_c", [65, 1088], BF16)
    st = sb_(ctx, "st", [128, 24], F32)
    sems = {e: ctx.enter_context(nc.semaphore("sem_" + e)) for e in ENGS}
    Z = smal[:, 1:2]
    EPS = smal[:, 0:1]

    hbuf = sb_(ctx2, "hbuf", [128, NFF * QT], BF16, side="right")
    wA = sb_(ctx2, "wA", [128, NC8 * FF], BF16, side="right")
    wB = sb_(ctx2, "wB", [128, NFF * D], BF16, side="right")
    gw_s = sb_(ctx2, "gw_s", [128, NC8 * 15], F32, side="right")
    rb1T = sb_(ctx2, "rb1T", [128, NFF * NE], F32, side="right")
    gated = sb_(ctx2, "gated", [128, NT * 15], F32, side="right")
    w15 = sb_(ctx2, "w15", [128, 15], F32, side="right")
    wzz = sb_(ctx2, "wzz", [128, 15], F32, side="right")
    wzm = sb_(ctx2, "wzm", [128, 15], F32, side="right")
    mx8 = sb_(ctx2, "mx8", [128, 8], F32, side="right")
    rb2row = sb_(ctx2, "rb2row", [1, D], BF16, side="right")
    pA = ps_(ctx2, "pA", [128, 1024])
    pC = ps_(ctx2, "pC", [128, 1024])
    pD = ps_(ctx2, "pD", [128, 1024])
    p0scr = sb_(ctx2, "p0scr", [128, 5120], BF16, side="right")
    xbuf = [p0scr[:, 0:2048].bitcast(F32), p0scr[:, 0:2048].bitcast(F32)]
    xbf = p0scr[:, 2048:3072]
    sqout = p0scr[:, 2048:3072]
    xnT32 = p0scr[:, 3072:5120].bitcast(F32)

    acc = sb_(ctxa, "acc", [128, NT * D], F32)
    biasrows = sb_(ctxb, "biasrows", [17, FF], BF16, side="right")

    P.op("dv", lambda v: v.memset(onesf[:], 1.0), writes=["onesf"])
    P.op("dv", lambda v: v.memset(smal[:, 0:1], 1e-5), writes=["smal"])
    P.op("dv", lambda v: v.memset(smal[:, 1:2], 0.0), writes=["smal"])
    P.op("dv", lambda v: v.memset(smal[:, 2:3], float(np.log(0.125 * 1.002))), writes=["smal"])
    P.op("dv", lambda v: v.memset(onesb[:], 1.0), writes=["onesb"])
    P.op("dv", lambda v: v.memset(onesbr[:], 1.0), writes=["onesbr"])
    for dst, src, r in [
            (idb[:], idb_d[:], "idb"), (idf[:], idf_d[:], "idf"),
            (rows_a[0:1, 0:D], dr["lnb"][None, :], "rows_a"),
            (rows_a[0:1, D:D + 15], dr["gb"][None, :], "rows_a"),
            (rows_b[0:1, 0:D], dr["bk"][None, :], "rows_b"),
            (rows_b[64:65, 0:D], dr["lng"][None, :], "rows_b"),
            (gw_s[:], dr["gw"].rearrange("(c p) e -> p c e", p=128), "gw_s"),
            ]:
        P.op("sq", lambda s, d=dst, i=src: s.dma_start(out=d, in_=i), writes=[r], inc=16)
    for dst, srcd in [(biasrows[0:15, :], dr["rb1"][:]),
                      (biasrows[15:16, :], dr["sb1"][0:1, 0:FF]),
                      (biasrows[16:17, :], dr["sb1"][0:1, FF:2 * FF])]:
        P.op("gq", lambda g, d=dst, i=srcd: g.dma_start(out=d, in_=i), writes=["brow"], inc=16)

    def _brt(t):
        for c in range(NFF):
            ins = t.matmul(pC[:, c * 17:(c + 1) * 17],
                           lhsT=biasrows[0:17, c * 128:(c + 1) * 128],
                           rhs=idb[0:17, 0:17], start=True, stop=True)
        return ins
    P.op("pe", _brt, reads=["brow", "idb"], writes=["pC0"])
    P.op("dv", lambda v: v.tensor_copy(rb1T[:, 0:NFF * NE], pC[:, 0:NFF * NE]),
         reads=["pC0"], writes=["rb1T"])

    def w1_src(e):
        if e < 15:
            return dr["rw1"][e:e + 1].rearrange("o (c p) f -> p (o c) f", p=128)
        return sw1_2d[:, (e - 15) * FF:(e - 14) * FF].rearrange("(c p) f -> p c f", p=128)

    def w2_src(e):
        if e < 15:
            return dr["rw2"][e:e + 1].rearrange("o (c p) f -> p (o c) f", p=128)
        return sw2_2d[(e - 15) * FF:(e - 14) * FF, :].rearrange("(c p) f -> p c f", p=128)

    P.op("gq", lambda g: g.dma_start(out=rows_c[0:1, 0:D], in_=dr["bq"][None, :]), writes=["rows_c"], inc=16)
    P.op("gq", lambda g: g.dma_start(out=rows_c[32:33, 0:D], in_=dr["bv"][None, :]), writes=["rows_c"], inc=16)
    P.op("gq", lambda g: g.dma_start(out=rows_c[64:65, 0:D], in_=dr["bo"][None, :]), writes=["rows_c"], inc=16)
    P.op("gq", lambda g: g.dma_start(out=wA[:], in_=w1_src(0)), writes=["wA"], inc=16)
    P.op("gq", lambda g: g.dma_start(out=wB[:], in_=w2_src(0)), writes=["wB"], inc=16)

    P.barrier()
    ctxb.close()

    if stop_after == "startup":
        P.final_wait("sq")
        P.replay(nc, sems)
        ctx2.close(); ctxa.close(); ctx.close()
        return _stage_and_run(nc, inputs, core_ids, trace)

    nt_lim = NT if stop_after != "tiles1" else 1

    gsc = [p0scr[:, 0:2048].bitcast(F32), p0scr[:, 2048:4096].bitcast(F32)]

    def emit_rb2row(e):
        if e < 16:
            rsrc = dr["rb2"][e:e + 1, :] if e < 15 else dr["sb2"][0:1, :]
            P.op("gq", lambda g, rsrc=rsrc: g.dma_start(out=rb2row[:], in_=rsrc),
                 writes=["rb2row"], inc=16)

    def emit_wprefetch(e):
        if e + 1 < NE:
            P.op("gq", lambda g, e=e: g.dma_start(out=wA[:], in_=w1_src(e + 1)),
                 writes=["wA"], inc=16)
            P.op("gq", lambda g, e=e: g.dma_start(out=wB[:], in_=w2_src(e + 1)),
                 writes=["wB"], inc=16)

    def emit_eq(e, q):
        if True:
            for f in range(NFF):
                ph = pA[:, 0:512] if f % 2 == 0 else pA[:, 512:1024]
                phr = "pA0" if f % 2 == 0 else "pA1"
                def _m1(t, f=f, q=q, ph=ph):
                    for c in range(NC8):
                        ins = t.matmul(ph, lhsT=wA[:, c * FF + f * 128:c * FF + (f + 1) * 128],
                                       rhs=xnT[:, c * T + q * QT:c * T + (q + 1) * QT],
                                       start=(c == 0), stop=(c == NC8 - 1))
                    return ins
                P.op("pe", _m1, reads=["wA", "xnT"], writes=[phr])
                P.op("ac", lambda s, f=f, e=e, ph=ph: s.activation(
                        hbuf[:, f * QT:(f + 1) * QT], ph, AF.Gelu,
                        bias=rb1T[:, f * NE + e:f * NE + e + 1], scale=1.0),
                     reads=[phr, "rb1T"], writes=[f"hb{f // 4}"])
            for j2 in range(4):
                J = q * 4 + j2
                pm = pC if J % 2 == 0 else pD
                pmr = None
                def _m2(t, j2=j2, pm=pm, e=e):
                    for cf in range(NFF):
                        for s_ in range(2):
                            ins = t.matmul(pm[:, s_ * 512:(s_ + 1) * 512],
                                           lhsT=hbuf[:, cf * QT + j2 * 128:cf * QT + (j2 + 1) * 128],
                                           rhs=wB[:, cf * D + s_ * 512:cf * D + (s_ + 1) * 512],
                                           start=(cf == 0), stop=(cf == NFF - 1 and e >= 16))
                    if e < 16:
                        for s_ in range(2):
                            ins = t.matmul(pm[:, s_ * 512:(s_ + 1) * 512], lhsT=onesbr[0:1, :],
                                           rhs=rb2row[0:1, s_ * 512:(s_ + 1) * 512],
                                           start=False, stop=True)
                    return ins
                pmrs = ["pC0", "pC1"] if J % 2 == 0 else ["pD0", "pD1"]
                mm2_reads = ["hb0", "hb1", "hb2", "hb3", "wB", "onesbr"]
                if e < 16:
                    mm2_reads.append("rb2row")
                P.op("pe", _m2, reads=mm2_reads, writes=pmrs)
                a_J = acc[:, J * D:(J + 1) * D]
                if e < 15:
                    P.op("dv", lambda v, a=a_J, J=J, e=e, pm=pm: v.scalar_tensor_tensor(
                            a, pm[:], gated[:, J * 15 + e:J * 15 + e + 1], a,
                            op0=ALU.mult, op1=ALU.add),
                         reads=pmrs + ["gated", f"acc{J % 4}"], writes=[f"acc{J % 4}"])
                else:
                    P.op("dv", lambda v, a=a_J, pm=pm: v.tensor_tensor(a, a, pm[:], op=ALU.add),
                         reads=pmrs + [f"acc{J % 4}"], writes=[f"acc{J % 4}"])
                if e == NE - 1:
                    gv = gsc[J % 2]
                    gr = f"p0g{J % 2}"
                    P.op("ac", lambda s, J=J, gv=gv: s.activation(gv, acc[:, J * D:(J + 1) * D],
                                                                  AF.Silu, bias=Z, scale=1.0),
                         reads=[f"acc{J % 4}", "smal"],
                         writes=[gr, "xb", "xbfr", "xnT32"])
                    P.op("gq", lambda g, J=J, gv=gv: g.dma_start(
                            out=g_dram[J * 128:(J + 1) * 128, :], in_=gv),
                         reads=[gr], writes=[f"gd{J}"], inc=16)



    def p0_load(j):
        if j >= nt_lim:
            return
        xb = xbuf[j % 2]
        P.op("sq", lambda s, xb=xb, j=j: s.dma_start(out=xb, in_=dr["xin"][j * 128:(j + 1) * 128, :]),
             writes=["xb"], inc=16)

    def p0_stats(j):
        if j >= nt_lim:
            return
        xb = xbuf[j % 2]
        xr = "xb"
        sb0 = 12 * (j % 2)
        sr = f"st{j % 2}"
        P.op("dv", lambda v, xb=xb, sb0=sb0: v.reduce_sum(st[:, sb0:sb0 + 1], xb, axis=AX.X),
             reads=[xr], writes=[sr])
        P.op("ac", lambda s, xb=xb, sb0=sb0: s.activation(sqout, xb, AF.Square, bias=Z, scale=1.0,
                                                 accum_out=st[:, sb0 + 2:sb0 + 3]),
             reads=[xr, "smal"], writes=["xbfr", sr])
        P.op("dv", lambda v, sb0=sb0: v.tensor_scalar_mul(st[:, sb0 + 1:sb0 + 2], st[:, sb0:sb0 + 1], 1.0 / D),
             reads=[sr], writes=[sr])
        P.op("dv", lambda v, sb0=sb0: v.tensor_tensor(st[:, sb0 + 5:sb0 + 6], st[:, sb0 + 1:sb0 + 2],
                                                      st[:, sb0 + 1:sb0 + 2], op=ALU.mult),
             reads=[sr], writes=[sr])
        P.op("dv", lambda v, sb0=sb0: v.scalar_tensor_tensor(st[:, sb0 + 6:sb0 + 7], st[:, sb0 + 2:sb0 + 3],
                                                             1.0 / D, st[:, sb0 + 5:sb0 + 6],
                                                             op0=ALU.mult, op1=ALU.subtract),
             reads=[sr], writes=[sr])
        P.op("ac", lambda s, sb0=sb0: s.activation(st[:, sb0 + 3:sb0 + 4], st[:, sb0 + 6:sb0 + 7], AF.Ln, bias=EPS, scale=1.0),
             reads=[sr, "smal"], writes=[sr])
        P.op("ac", lambda s, sb0=sb0: s.activation(st[:, sb0 + 4:sb0 + 5], st[:, sb0 + 3:sb0 + 4], AF.Exp, bias=Z, scale=-0.5),
             reads=[sr, "smal"], writes=[sr])
        P.op("dv", lambda v, sb0=sb0: v.tensor_tensor(st[:, sb0 + 7:sb0 + 8], st[:, sb0 + 1:sb0 + 2],
                                                      st[:, sb0 + 4:sb0 + 5], op=ALU.mult),
             reads=[sr], writes=[sr])
        P.op("dv", lambda v, sb0=sb0: v.tensor_scalar_mul(st[:, sb0 + 9:sb0 + 10], st[:, sb0 + 7:sb0 + 8], -1.0),
             reads=[sr], writes=[sr])

    def p0_apply(j):
        xb = xbuf[j % 2]
        xr = "xb"
        a_j = acc[:, j * D:(j + 1) * D]
        ar = f"acc{j % 4}"
        sb0 = 12 * (j % 2)
        sr = f"st{j % 2}"
        pTR = pA if j % 2 == 0 else pD
        pTRr = ["pA0", "pA1"] if j % 2 == 0 else ["pD0", "pD1"]
        P.op("ac", lambda s, xb=xb, a=a_j, sb0=sb0: s.activation(a, xb, AF.Identity,
                                                bias=st[:, sb0 + 9:sb0 + 10], scale=st[:, sb0 + 4:sb0 + 5]),
             reads=[xr, sr], writes=[ar])
        P.op("ac", lambda s, a=a_j: s.activation(xbf, a, AF.Identity, bias=Z, scale=1.0),
             reads=[ar, "smal"], writes=["xbfr"])
        def _tr(t, pTR=pTR):
            for c in range(NC8):
                ins = t.matmul(pTR[:, c * 128:(c + 1) * 128], lhsT=xbf[:, c * 128:(c + 1) * 128],
                               rhs=idb[:], start=True, stop=True)
            return ins
        P.op("pe", _tr, reads=["xbfr", "idb"], writes=pTRr)
        xnT_view = xnT.ap().rearrange("p (c t) -> p c t", c=NC8)[:, :, j * 128:(j + 1) * 128]
        P.op("ac", lambda s, v_=xnT_view, pTR=pTR: s.activation(v_, pTR[:], AF.Identity, bias=Z, scale=1.0),
             reads=pTRr + ["smal"], writes=["xnT"])

    def p1_gate(j):
        a_j = acc[:, j * D:(j + 1) * D]
        ar = f"acc{j % 4}"
        sb0 = 12 * (j % 2)
        sr = f"st{j % 2}"
        pXR = pD if j % 2 == 0 else pA
        pXRr = ["pD0", "pD1"] if j % 2 == 0 else ["pA0", "pA1"]
        pG = pC[:, 0:15] if j % 2 == 0 else pC[:, 512:527]
        pGr = "pC0" if j % 2 == 0 else "pC1"
        def _xtr32(t, j=j, pXR=pXR):
            for c in range(NC8):
                ins = t.matmul(pXR[:, c * 128:(c + 1) * 128],
                               lhsT=acc[:, j * D + c * 128:j * D + (c + 1) * 128],
                               rhs=idf[:], start=True, stop=True)
            return ins
        P.op("pe", _xtr32, reads=[ar, "idf"], writes=pXRr)
        P.op("dv", lambda v, pXR=pXR: v.tensor_copy(xnT32[:], pXR[:]), reads=pXRr, writes=["xnT32"])
        def _gmm(t, pG=pG):
            for c in range(NC8):
                t.matmul(pG, lhsT=xnT32[:, c * 128:(c + 1) * 128],
                         rhs=gw_s[:, c * 15:(c + 1) * 15], start=(c == 0), stop=False)
            return t.matmul(pG, lhsT=onesf[0:1, :], rhs=rows_a[0:1, D:D + 15],
                            start=False, stop=True)
        P.op("pe", _gmm, reads=["xnT32", "gw_s", "rows_a", "onesf"], writes=[pGr])
        P.op("dv", lambda v, pG=pG, sb0=sb0: v.reduce_max(st[:, sb0 + 5:sb0 + 6], pG, axis=AX.X),
             reads=[pGr], writes=[sr])
        P.op("dv", lambda v, sb0=sb0: v.tensor_scalar_mul(st[:, sb0 + 6:sb0 + 7], st[:, sb0 + 5:sb0 + 6], -1.0),
             reads=[sr], writes=[sr])
        P.op("ac", lambda s, pG=pG, sb0=sb0: s.activation(w15[:], pG, AF.Exp, bias=st[:, sb0 + 6:sb0 + 7], scale=1.0,
                                          accum_out=st[:, sb0 + 7:sb0 + 8]),
             reads=[pGr, sr], writes=["w15", sr])
        P.op("dv", lambda v, sb0=sb0: v.reciprocal(st[:, sb0 + 8:sb0 + 9], st[:, sb0 + 7:sb0 + 8]),
             reads=[sr], writes=[sr])
        P.op("dv", lambda v, sb0=sb0: v.tensor_scalar(w15[:], w15[:], st[:, sb0 + 8:sb0 + 9], None, op0=ALU.mult),
             reads=["w15", sr], writes=["w15"])
        P.op("dv", lambda v: v.max(out=mx8[:], in_=w15[:]), reads=["w15"], writes=["mx8"])
        P.op("dv", lambda v: v.memset(mx8[:, 3:8], -1e30), writes=["mx8"])
        P.op("dv", lambda v: v.match_replace(out=wzz[:], in_to_replace=mx8[:], in_values=w15[:],
                                             imm_value=-1e30),
             reads=["mx8", "w15"], writes=["wzz", "mx8"])
        P.op("dv", lambda v: v.tensor_tensor(wzm[:], w15[:], wzz[:], op=ALU.not_equal),
             reads=["w15", "wzz"], writes=["wzm"])
        P.op("dv", lambda v, j=j: v.tensor_tensor(gated[:, j * 15:(j + 1) * 15], wzm[:], w15[:],
                                                  op=ALU.mult),
             reads=["wzm", "w15"], writes=["gated"])

    p0_load(0)
    for j in range(nt_lim):
        p0_stats(j)
        p0_apply(j)
        p0_load(j + 1)
        p1_gate(j)
        if j % 4 == 3 and stop_after != "p1":
            if j == 3:
                emit_rb2row(0)
            emit_eq(0, j // 4)
            if j == 15:
                emit_wprefetch(0)

    tap("xnT", lambda: xnT[:], ["xnT"])
    tap("acc1", lambda: acc[:], ["acc0", "acc1", "acc2", "acc3"])
    tap("gated", lambda: gated[:], ["gated"])
    if stop_after == "tiles1":
        P.final_wait("sq")
        P.replay(nc, sems)
        ctx2.close(); ctxa.close(); ctx.close()
        return _stage_and_run(nc, inputs, core_ids, trace)

    for e in (range(1, NE) if stop_after != "p1" else []):
        emit_rb2row(e)
        for q in range(NQ):
            emit_eq(e, q)
        emit_wprefetch(e)

    tap("accf", lambda: acc[:], ["acc0", "acc1", "acc2", "acc3"])

    if stop_after in ("p1", "p2"):
        P.final_wait("sq")
        P.replay(nc, sems)
        ctxb.close(); ctx2.close(); ctxa.close(); ctx.close()
        return _stage_and_run(nc, inputs, core_ids, trace)

    P.barrier()
    ctx2.close()

    ctx3 = ExitStack()
    khT = sb_(ctx3, "khT", [128, NC8 * T], BF16, side="right")
    vtm = sb_(ctx3, "vtm", [128, NT * 1040], BF16, side="right")
    wq_s = sb_(ctx3, "wq_s", [128, NC8 * D], BF16, side="right")
    wo_s = sb_(ctx3, "wo_s", [128, NC8 * D], BF16, side="right")
    bkT = sb_(ctx3, "bkT", [128, 8], F32, side="right")
    psS = ctx3.enter_context(nc.psum_tensor("psS", [128, 2048], F32))
    psQ = ctx3.enter_context(nc.psum_tensor("psQ", [128, 1024], F32))
    psT = ctx3.enter_context(nc.psum_tensor("psT", [128, 1024], F32))
    sunit = [psS[:, 0:1024], psS[:, 1024:2048], psT[:, 0:1024]]

    P.op("gq", lambda g: g.dma_start(out=wq_s[:], in_=dr["wk"].rearrange("(c p) f -> p c f", p=128)),
         writes=["wq_s"], inc=16)
    P.op("gq", lambda g: g.dma_start(out=wo_s[:], in_=dr["wv"].rearrange("(c p) f -> p c f", p=128)),
         writes=["wo_s"], inc=16)

    def _bkt(t):
        for c in range(NC8):
            ins = t.matmul(psT[:, c:c + 1], lhsT=rows_b[0:1, c * 128:(c + 1) * 128],
                           rhs=onesf[0:1, 0:1], start=True, stop=True)
        return ins
    P.op("pe", _bkt, reads=["rows_b", "onesf"], writes=["psU2"])
    P.op("dv", lambda v: v.tensor_copy(bkT[:], psT[:, 0:8]), reads=["psU2"], writes=["bkT"])

    for cof in range(NC8):
        for hf in range(2):
            flip = (cof * 2 + hf) % 2 == 0
            pk = psQ if flip else psT
            pkw = ["psQa", "psQb"] if flip else ["psU2"]
            def _kmm(t, cof=cof, hf=hf, pk=pk):
                for c in range(NC8):
                    for s_ in range(2):
                        ins = t.matmul(
                            pk[:, s_ * 512:(s_ + 1) * 512],
                            lhsT=wq_s[:, c * D + cof * 128:c * D + (cof + 1) * 128],
                            rhs=xnT[:, c * T + hf * 1024 + s_ * 512:c * T + hf * 1024 + (s_ + 1) * 512],
                            start=(c == 0), stop=(c == NC8 - 1))
                return ins
            P.op("pe", _kmm, reads=["wq_s", "xnT"], writes=pkw)
            P.op("ac", lambda s, cof=cof, hf=hf, pk=pk: s.activation(
                    khT[:, cof * T + hf * 1024:cof * T + (hf + 1) * 1024], pk[:], AF.Identity,
                    bias=bkT[:, cof:cof + 1], scale=1.0),
                 reads=pkw + ["bkT"], writes=["khT"])
    P.op("dv", lambda v: v.memset(vtm[:], 1.0), writes=["vtm"])
    for kt in range(NT):
        flip = kt % 2 == 0
        pk = psQ if flip else psT
        pkw = ["psQa", "psQb"] if flip else ["psU2"]
        def _vmm(t, kt=kt, pk=pk):
            for s_ in range(2):
                for c in range(NC8):
                    t.matmul(pk[:, s_ * 512:(s_ + 1) * 512],
                             lhsT=xnT[:, c * T + kt * 128:c * T + (kt + 1) * 128],
                             rhs=wo_s[:, c * D + s_ * 512:c * D + (s_ + 1) * 512],
                             start=(c == 0), stop=False)
                ins = t.matmul(pk[:, s_ * 512:(s_ + 1) * 512], lhsT=onesbr[32:33, :],
                               rhs=rows_c[32:33, s_ * 512:(s_ + 1) * 512], start=False, stop=True)
            return ins
        P.op("pe", _vmm, reads=["xnT", "wo_s", "rows_c", "onesbr"], writes=pkw)
        vv = vtm.ap().rearrange("p (kt h e) -> p kt h e", kt=NT, h=H)[:, kt, :, 0:64]
        P.op("ac", lambda s, vv=vv, pk=pk: s.activation(vv, pk[:], AF.Identity, bias=Z, scale=1.0),
             reads=pkw + ["smal"], writes=["vtm"])

    P.op("gq", lambda g: g.dma_start(out=wq_s[:], in_=dr["wq"].rearrange("(c p) f -> p c f", p=128)),
         writes=["wq_s"], inc=16)
    P.op("gq", lambda g: g.dma_start(out=wo_s[:], in_=dr["wo"].rearrange("(c p) f -> p c f", p=128)),
         writes=["wo_s"], inc=16)

    tap("khT", lambda: khT[:], ["khT"])
    tap("vtm", lambda: vtm[:], ["vtm"])

    if stop_after == "p3":
        P.final_wait("sq")
        P.replay(nc, sems)
        ctxb.close(); ctx3.close(); ctxa.close(); ctx.close()
        return _stage_and_run(nc, inputs, core_ids, trace)

    P.barrier()
    ctxa.close()

    ctx4 = ExitStack()
    qnP = [sb_(ctx4, "qn0t", [128, D], F32), sb_(ctx4, "qn1t", [128, D], F32)]
    qnbP = [sb_(ctx4, "qnb0", [128, D], BF16), sb_(ctx4, "qnb1", [128, D], BF16)]
    qTbP = [sb_(ctx4, "qTb0", [128, D], BF16), sb_(ctx4, "qTb1", [128, D], BF16)]
    expST = sb_(ctx4, "expST", [128, 4096], BF16)
    snk = sb_(ctx4, "snk", [128, D], BF16)
    ao = sb_(ctx4, "ao", [128, D], F32)
    abf = sb_(ctx4, "abf", [128, D], BF16)
    aTb = sb_(ctx4, "aTb", [128, D], BF16)
    xg = sb_(ctx4, "xg", [128, 2 * D], F32)
    ob = sb_(ctx4, "ob", [128, D], F32)
    gbc = sb_(ctx4, "gbc", [128, D], F32)
    bbc = sb_(ctx4, "bbc", [128, D], F32)
    mh = sb_(ctx4, "mh", [128, 16], F32)
    qst = sb_(ctx4, "qst", [128, 16], F32)

    def _gbcmm(t):
        for s_ in range(2):
            ins = t.matmul(psQ[:, s_ * 512:(s_ + 1) * 512], lhsT=onesf[64:65, :],
                           rhs=rows_b[64:65, s_ * 512:(s_ + 1) * 512], start=True, stop=True)
        return ins
    P.op("pe", _gbcmm, reads=["rows_b", "onesf"], writes=["psQa", "psQb"])
    P.op("dv", lambda v: v.tensor_copy(gbc[:], psQ[:]), reads=["psQa", "psQb"], writes=["gbc"])
    def _bbcmm(t):
        for s_ in range(2):
            ins = t.matmul(psQ[:, s_ * 512:(s_ + 1) * 512], lhsT=onesf[0:1, :],
                           rhs=rows_a[0:1, s_ * 512:(s_ + 1) * 512], start=True, stop=True)
        return ins
    P.op("pe", _bbcmm, reads=["rows_a", "onesf"], writes=["psQa", "psQb"])
    P.op("dv", lambda v: v.tensor_copy(bbc[:], psQ[:]), reads=["psQa", "psQb"], writes=["bbc"])

    unext = [0]
    def take_unit():
        u = unext[0] % 3
        unext[0] += 1
        return sunit[u], f"psU{u}"

    def emit_qmm_ln(jj):
        pp = jj % 2
        qn_, qnb_ = qnP[pp], qnbP[pp]
        rq, rqb = f"qn{pp}", f"qnb{pp}"
        sq0 = 8 * pp
        rst = f"qst{pp}"
        pS, rS = take_unit()
        def _qmm(t, jj=jj, pS=pS):
            for s_ in range(2):
                for c in range(NC8):
                    t.matmul(pS[:, s_ * 512:(s_ + 1) * 512],
                             lhsT=xnT[:, c * T + jj * 128:c * T + (jj + 1) * 128],
                             rhs=wq_s[:, c * D + s_ * 512:c * D + (s_ + 1) * 512],
                             start=(c == 0), stop=False)
                ins = t.matmul(pS[:, s_ * 512:(s_ + 1) * 512], lhsT=onesbr[0:1, :],
                               rhs=rows_c[0:1, s_ * 512:(s_ + 1) * 512], start=False, stop=True)
            return ins
        P.op("pe", _qmm, reads=["xnT", "wq_s", "rows_c", "onesbr"], writes=[rS])
        P.op("dv", lambda v, pS=pS, sq0=sq0: v.reduce_sum(qst[:, sq0:sq0 + 1], pS[:], axis=AX.X),
             reads=[rS], writes=[rst])
        P.op("ac", lambda s, pS=pS, sq0=sq0: s.activation(snk[:], pS[:], AF.Square, bias=Z, scale=1.0,
                                                          accum_out=qst[:, sq0 + 1:sq0 + 2]),
             reads=[rS, "smal"], writes=["snk", rst])
        P.op("dv", lambda v, sq0=sq0: v.tensor_scalar_mul(qst[:, sq0 + 2:sq0 + 3], qst[:, sq0:sq0 + 1], 1.0 / D),
             reads=[rst], writes=[rst])
        P.op("dv", lambda v, sq0=sq0: v.tensor_tensor(qst[:, sq0 + 3:sq0 + 4], qst[:, sq0 + 2:sq0 + 3],
                                                      qst[:, sq0 + 2:sq0 + 3], op=ALU.mult),
             reads=[rst], writes=[rst])
        P.op("dv", lambda v, sq0=sq0: v.scalar_tensor_tensor(qst[:, sq0 + 4:sq0 + 5], qst[:, sq0 + 1:sq0 + 2],
                                                             1.0 / D, qst[:, sq0 + 3:sq0 + 4],
                                                             op0=ALU.mult, op1=ALU.subtract),
             reads=[rst], writes=[rst])
        P.op("ac", lambda s, sq0=sq0: s.activation(qst[:, sq0 + 5:sq0 + 6], qst[:, sq0 + 4:sq0 + 5],
                                                   AF.Ln, bias=EPS, scale=1.0),
             reads=[rst, "smal"], writes=[rst])
        P.op("ac", lambda s, sq0=sq0: s.activation(qst[:, sq0 + 6:sq0 + 7], qst[:, sq0 + 5:sq0 + 6],
                                                   AF.Exp, bias=Z, scale=-0.5),
             reads=[rst, "smal"], writes=[rst])
        P.op("dv", lambda v, sq0=sq0: v.tensor_tensor(qst[:, sq0 + 7:sq0 + 8], qst[:, sq0 + 2:sq0 + 3],
                                                      qst[:, sq0 + 6:sq0 + 7], op=ALU.mult),
             reads=[rst], writes=[rst])
        P.op("dv", lambda v, sq0=sq0: v.tensor_scalar_mul(qst[:, sq0 + 7:sq0 + 8], qst[:, sq0 + 7:sq0 + 8], -1.0),
             reads=[rst], writes=[rst])
        P.op("ac", lambda s, qn_=qn_, pS=pS, sq0=sq0: s.activation(
                qn_[:], pS[:], AF.Identity, bias=qst[:, sq0 + 7:sq0 + 8], scale=qst[:, sq0 + 6:sq0 + 7]),
             reads=[rS, rst], writes=[rq])
        P.op("dv", lambda v, qn_=qn_: v.tensor_tensor(qn_[:], qn_[:], gbc[:], op=ALU.mult),
             reads=[rq, "gbc"], writes=[rq])
        P.op("dv", lambda v, qn_=qn_: v.tensor_tensor(qn_[:], qn_[:], bbc[:], op=ALU.add),
             reads=[rq, "bbc"], writes=[rq])
        P.op("dv", lambda v, qn_=qn_, qnb_=qnb_: v.tensor_copy(qnb_[:], qn_[:]), reads=[rq], writes=[rqb])

    def emit_qtr(jj):
        pp = jj % 2
        qnb_, qTb_ = qnbP[pp], qTbP[pp]
        rqb, rqt = f"qnb{pp}", f"qTb{pp}"
        pS, rS = take_unit()
        def _qtr(t, pS=pS, qnb_=qnb_):
            for c in range(NC8):
                ins = t.matmul(pS[:, c * 128:(c + 1) * 128], lhsT=qnb_[:, c * 128:(c + 1) * 128],
                               rhs=idb[:], start=True, stop=True)
            return ins
        P.op("pe", _qtr, reads=[rqb, "idb"], writes=[rS])
        P.op("dv", lambda v, pS=pS, qTb_=qTb_: v.tensor_copy(qTb_[:], pS[:]), reads=[rS], writes=[rqt])

    def emit_av(h):
        par = h % 2
        avs = psQ[:, 0:65] if par == 0 else psQ[:, 512:577]
        avr = "psQa" if par == 0 else "psQb"
        def _av(t, h=h, avs=avs, par=par):
            for kt in range(NT):
                ins = t.matmul(avs, lhsT=expST[:, par * 2048 + kt * 128:par * 2048 + (kt + 1) * 128],
                               rhs=vtm[:, kt * 1040 + h * 65:kt * 1040 + (h + 1) * 65],
                               start=(kt == 0), stop=(kt == NT - 1))
            return ins
        P.op("pe", _av, reads=["expST" + ("a" if par == 0 else "b"), "vtm"], writes=[avr])
        mhr = f"mh{par}"
        P.op("dv", lambda v, h=h, avs=avs: v.tensor_scalar(mh[:, 4 + (h % 2):5 + (h % 2)],
                                                           avs[:, 64:65], 1e-8, None, op0=ALU.add),
             reads=[avr], writes=[mhr])
        P.op("dv", lambda v, h=h: v.reciprocal(mh[:, 8 + (h % 2):9 + (h % 2)],
                                               mh[:, 4 + (h % 2):5 + (h % 2)]),
             reads=[mhr], writes=[mhr])
        P.op("dv", lambda v, h=h, avs=avs: v.tensor_scalar(
                ao[:, h * 64:(h + 1) * 64], avs[:, 0:64],
                mh[:, 8 + (h % 2):9 + (h % 2)], None, op0=ALU.mult),
             reads=[avr, mhr], writes=["ao"])

    emit_qmm_ln(0)
    emit_qtr(0)
    for j in range(NT):
        pp = j % 2
        qn_, qTb_ = qnP[pp], qTbP[pp]
        rq, rqt = f"qn{pp}", f"qTb{pp}"
        xv = xg[:, 0:D]
        gv4 = xg[:, D:2 * D]
        xrr, grr = "x4", "g4"
        P.op("sq", lambda s, j=j, xv=xv: s.dma_start(out=xv, in_=dr["xin"][j * 128:(j + 1) * 128, :]),
             writes=[xrr], inc=16)
        P.op("sq", lambda s, j=j, gv=gv4: s.dma_start(out=gv, in_=g_dram[j * 128:(j + 1) * 128, :]),
             reads=[f"gd{j}"], writes=[grr], inc=16)
        for h in range(H):
            poff = 64 * (h % 2)
            coff = (h // 2) * 128
            par = h % 2
            for kh in range(2):
                pS, rS = take_unit()
                def _stmm(t, poff=poff, coff=coff, h=h, kh=kh, pS=pS, qTb_=qTb_):
                    for kt in range(8):
                        ktg = kh * 8 + kt
                        ins = t.matmul(pS[:, kt * 128:(kt + 1) * 128],
                                       lhsT=khT[poff:poff + 64, (h // 2) * T + ktg * 128:(h // 2) * T + (ktg + 1) * 128],
                                       rhs=qTb_[poff:poff + 64, coff:coff + 128],
                                       start=True, stop=True)
                    return ins
                P.op("pe", _stmm, reads=[rqt, "khT"], writes=[rS])
                P.op("ac", lambda s, par=par, kh=kh, pS=pS: s.activation(
                        expST[:, par * 2048 + kh * 1024:par * 2048 + (kh + 1) * 1024], pS[:],
                        AF.Exp, bias=Z, scale=0.125),
                     reads=[rS, "smal"], writes=["expST" + ("a" if par == 0 else "b")])
            if h > 0:
                emit_av(h - 1)
            if h == 7 and j + 1 < NT:
                emit_qmm_ln(j + 1)
            if h == 11 and j + 1 < NT:
                emit_qtr(j + 1)
        emit_av(H - 1)
        P.op("dv", lambda v, qn_=qn_: v.tensor_tensor(abf[:], ao[:], qn_[:], op=ALU.add),
             reads=["ao", rq], writes=["abf"])
        pS, rS = take_unit()
        def _atr(t, pS=pS):
            for c in range(NC8):
                ins = t.matmul(pS[:, c * 128:(c + 1) * 128], lhsT=abf[:, c * 128:(c + 1) * 128],
                               rhs=idb[:], start=True, stop=True)
            return ins
        P.op("pe", _atr, reads=["abf", "idb"], writes=[rS])
        P.op("dv", lambda v, pS=pS: v.tensor_copy(aTb[:], pS[:]), reads=[rS], writes=["aTb"])
        def _omm(t):
            for s_ in range(2):
                for c in range(NC8):
                    t.matmul(psQ[:, s_ * 512:(s_ + 1) * 512], lhsT=aTb[:, c * 128:(c + 1) * 128],
                             rhs=wo_s[:, c * D + s_ * 512:c * D + (s_ + 1) * 512],
                             start=(c == 0), stop=False)
                ins = t.matmul(psQ[:, s_ * 512:(s_ + 1) * 512], lhsT=onesbr[64:65, :],
                               rhs=rows_c[64:65, s_ * 512:(s_ + 1) * 512], start=False, stop=True)
            return ins
        P.op("pe", _omm, reads=["aTb", "wo_s", "rows_c", "onesbr"], writes=["psQa", "psQb"])
        P.op("dv", lambda v, gv=gv4: v.tensor_tensor(ob[:], psQ[:], gv, op=ALU.mult),
             reads=["psQa", "psQb", grr], writes=["obr"])
        P.op("dv", lambda v, xv=xv: v.tensor_tensor(ob[:], ob[:], xv, op=ALU.add),
             reads=["obr", xrr], writes=["obr"])
        P.op("sq", lambda s, j=j: s.dma_start(out=out_d[j * 128:(j + 1) * 128, :], in_=ob[:]),
             reads=["obr"], inc=16)
        if j == 0:
            tap("qn0", lambda: qnP[0][:], ["qn0"])
            tap("ao0", lambda: ao[:], ["ao"])

    P.final_wait("sq")
    P.replay(nc, sems)
    ctx4.close()
    ctx3.close()
    ctx.close()
    return _stage_and_run(nc, inputs, core_ids, trace)


def _stage_and_run(nc, inputs, core_ids, trace):
    x_full = np.ascontiguousarray(np.asarray(inputs["L0_attn_x"], dtype=np.float32))
    base = {
        "wq": inputs["L0_attn_Wq"], "bq": inputs["L0_attn_bq"],
        "wk": inputs["L0_attn_Wk"], "bk": inputs["L0_attn_bk"],
        "wv": inputs["L0_attn_Wv"], "bv": inputs["L0_attn_bv"],
        "wo": inputs["L0_attn_Wo"], "bo": inputs["L0_attn_bo"],
        "lng": inputs["L0_attn_ln_g"], "lnb": inputs["L0_attn_ln_b"],
        "sw1": inputs["L0_ffn_sW1"], "sb1": inputs["L0_ffn_sb1"],
        "sw2": inputs["L0_ffn_sW2"], "sb2": inputs["L0_ffn_sb2"],
        "rw1": inputs["L0_ffn_rW1"], "rb1": inputs["L0_ffn_rb1"],
        "rw2": inputs["L0_ffn_rW2"], "rb2": inputs["L0_ffn_rb2"],
        "gw": inputs["L0_ffn_gW"], "gb": inputs["L0_ffn_gb"],
    }
    base = {k: np.ascontiguousarray(np.asarray(v, dtype=np.float32)) for k, v in base.items()}
    base["idb"] = np.eye(128, dtype=BF)
    base["idf"] = np.eye(128, dtype=np.float32)
    in_maps = []
    for c in range(len(core_ids)):
        m = dict(base)
        m["xin"] = np.ascontiguousarray(x_full[c])
        in_maps.append(m)
    return run_bass_kernel_spmd(nc, in_maps, core_ids=core_ids, trace=trace)


def kernel(**inputs):
    res = build_and_run(inputs)
    return np.stack([r["xout"] for r in res.results], axis=0).astype(np.float32)


# version 23
# speedup vs baseline: 1.0124x; 1.0069x over previous
import numpy as np
import ml_dtypes
from contextlib import ExitStack

import concourse.bass as bass
import concourse.mybir as mybir
from concourse.bass_utils import run_bass_kernel_spmd

F32 = mybir.dt.float32
BF16 = mybir.dt.bfloat16
AF = mybir.ActivationFunctionType
ALU = mybir.AluOpType
AX = mybir.AxisListType
BF = ml_dtypes.bfloat16

T = 2048
D = 1024
NT = 16
NC8 = 8
H = 16
FF = 2048
NFF = 16
NE = 17
QT = 512
NQ = 4

ENGS = ("pe", "dv", "ac", "gq", "sq")
BLOCK_OF = {"pe": "tensor", "dv": "vector", "ac": "scalar", "gq": "gpsimd", "sq": "sync"}


class Prog:

    def __init__(self):
        self.cmds = {e: [] for e in ENGS}
        self.cnt = dict.fromkeys(ENGS, 0)
        self.hwm = {e: {} for e in ENGS}
        self.res = {}

    def _wait(self, eng, tok):
        p, v = tok
        if p == eng == "pe":
            return
        if v <= self.hwm[eng].get(p, 0):
            return
        self.hwm[eng][p] = v
        self.cmds[eng].append(("w", p, v))

    def op(self, eng, fn, reads=(), writes=(), inc=1):
        toks = []
        for r in reads:
            e = self.res.get(r)
            if e is not None and e[0] is not None:
                toks.append(e[0])
        for w in writes:
            e = self.res.get(w)
            if e is not None:
                if e[0] is not None:
                    toks.append(e[0])
                toks.extend(e[1])
        for t in sorted(set(toks)):
            self._wait(eng, t)
        self.cnt[eng] += inc
        tok = (eng, self.cnt[eng])
        self.cmds[eng].append(("e", fn, inc))
        if inc == 16:
            self.hwm[eng][eng] = self.cnt[eng]
            self.cmds[eng].append(("w", eng, self.cnt[eng]))
        for r in reads:
            self.res.setdefault(r, [None, []])[1].append(tok)
        for w in writes:
            self.res[w] = [tok, []]
        return tok

    def barrier(self):
        toks = [(e, self.cnt[e]) for e in ENGS if self.cnt[e] > 0]
        for e in ENGS:
            for t in toks:
                if t[0] != e:
                    self._wait(e, t)

    def final_wait(self, eng):
        self.cmds[eng].append(("w", eng, self.cnt[eng]))

    def replay(self, nc, sems):
        with nc.Block() as block:
            for eng in ENGS:
                def body(E, eng=eng):
                    for c in self.cmds[eng]:
                        if c[0] == "w":
                            E.wait_ge(sems[c[1]], c[2])
                        else:
                            c[1](E).then_inc(sems[eng], c[2])
                getattr(block, BLOCK_OF[eng])(body)


def build_and_run(inputs, taps=(), core_ids=None, trace=False, stop_after=None):
    if core_ids is None:
        core_ids = list(range(8))

    nc = bass.Bass("TRN2", target_bir_lowering=False)
    P = Prog()

    dr = {}
    for n, s in [("xin", (T, D)),
                 ("wq", (D, D)), ("bq", (D,)), ("wk", (D, D)), ("bk", (D,)),
                 ("wv", (D, D)), ("bv", (D,)), ("wo", (D, D)), ("bo", (D,)),
                 ("lng", (D,)), ("lnb", (D,)),
                 ("sw1", (1, D, 4096)), ("sb1", (1, 4096)), ("sw2", (1, 4096, D)),
                 ("sb2", (1, D)), ("rw1", (15, D, FF)), ("rb1", (15, FF)),
                 ("rw2", (15, FF, D)), ("rb2", (15, D)), ("gw", (D, 15)), ("gb", (15,))]:
        dr[n] = nc.dram_tensor(n, list(s), F32, kind="ExternalInput").ap()
    idb_d = nc.dram_tensor("idb", [128, 128], BF16, kind="ExternalInput").ap()
    idf_d = nc.dram_tensor("idf", [128, 128], F32, kind="ExternalInput").ap()
    out_d = nc.dram_tensor("xout", [T, D], F32, kind="ExternalOutput").ap()
    g_dram = nc.dram_tensor("gsilu", [T, D], F32, kind="Internal").ap()
    sw1_2d = dr["sw1"].rearrange("o d f -> (o d) f")
    sw2_2d = dr["sw2"].rearrange("o f d -> (o f) d")

    tap_d = {}
    for tname, tshape, tdt in taps:
        tap_d[tname] = nc.dram_tensor("tap_" + tname, list(tshape), tdt,
                                      kind="ExternalOutput").ap()

    def tap(name, src_fn, reads):
        if name in tap_d:
            P.op("sq", lambda s, f=src_fn, n=name: s.dma_start(out=tap_d[n][:], in_=f()),
                 reads=reads, inc=16)

    ctx = ExitStack()
    ctx2 = ExitStack()
    ctxa = ExitStack()
    ctxb = ExitStack()
    sb_ = lambda cx, name, shape, dt, side="left": cx.enter_context(nc.sbuf_tensor(name, shape, dt, side=side))
    ps_ = lambda cx, name, shape: cx.enter_context(nc.psum_tensor(name, shape, F32))

    xnT = sb_(ctx, "xnT", [128, NC8 * T], BF16)
    idb = sb_(ctx, "idbs", [128, 128], BF16)
    idf = sb_(ctx, "idfs", [128, 128], F32)
    onesf = sb_(ctx, "onesf", [65, 128], F32)
    rows_a = sb_(ctx, "rows_a", [1, 1088], F32)
    rows_b = sb_(ctx, "rows_b", [65, 1088], F32)
    smal = sb_(ctx, "smal", [128, 8], F32)
    onesb = sb_(ctx, "onesb", [128, 1], BF16)
    onesbr = sb_(ctx, "onesbr", [65, 128], BF16)
    rows_c = sb_(ctx, "rows_c", [65, 1088], BF16)
    st = sb_(ctx, "st", [128, 24], F32)
    sems = {e: ctx.enter_context(nc.semaphore("sem_" + e)) for e in ENGS}
    Z = smal[:, 1:2]
    EPS = smal[:, 0:1]

    hbuf = sb_(ctx2, "hbuf", [128, NFF * QT], BF16, side="right")
    wA = sb_(ctx2, "wA", [128, NC8 * FF], BF16, side="right")
    wB = sb_(ctx2, "wB", [128, NFF * D], BF16, side="right")
    gw_s = sb_(ctx2, "gw_s", [128, NC8 * 15], F32, side="right")
    rb1T = sb_(ctx2, "rb1T", [128, NFF * NE], F32, side="right")
    gated = sb_(ctx2, "gated", [128, NT * 15], F32, side="right")
    w15 = sb_(ctx2, "w15", [128, 15], F32, side="right")
    wzz = sb_(ctx2, "wzz", [128, 15], F32, side="right")
    wzm = sb_(ctx2, "wzm", [128, 15], F32, side="right")
    mx8 = sb_(ctx2, "mx8", [128, 8], F32, side="right")
    rb2row = sb_(ctx2, "rb2row", [1, D], BF16, side="right")
    pA = ps_(ctx2, "pA", [128, 1024])
    pC = ps_(ctx2, "pC", [128, 1024])
    pD = ps_(ctx2, "pD", [128, 1024])
    p0scr = sb_(ctx2, "p0scr", [128, 5120], BF16, side="right")
    xbuf = [p0scr[:, 0:2048].bitcast(F32), p0scr[:, 0:2048].bitcast(F32)]
    xbf = p0scr[:, 2048:3072]
    sqout = p0scr[:, 2048:3072]
    xnT32 = p0scr[:, 3072:5120].bitcast(F32)

    acc = sb_(ctxa, "acc", [128, NT * D], F32)
    biasrows = sb_(ctxb, "biasrows", [17, FF], BF16, side="right")

    P.op("dv", lambda v: v.memset(onesf[:], 1.0), writes=["onesf"])
    P.op("dv", lambda v: v.memset(smal[:, 0:1], 1e-5), writes=["smal"])
    P.op("dv", lambda v: v.memset(smal[:, 1:2], 0.0), writes=["smal"])
    P.op("dv", lambda v: v.memset(smal[:, 2:3], float(np.log(0.125 * 1.002))), writes=["smal"])
    P.op("dv", lambda v: v.memset(onesb[:], 1.0), writes=["onesb"])
    P.op("dv", lambda v: v.memset(onesbr[:], 1.0), writes=["onesbr"])
    for dst, src, r in [
            (idb[:], idb_d[:], "idb"), (idf[:], idf_d[:], "idf"),
            (rows_a[0:1, 0:D], dr["lnb"][None, :], "rows_a"),
            (rows_a[0:1, D:D + 15], dr["gb"][None, :], "rows_a"),
            (rows_b[0:1, 0:D], dr["bk"][None, :], "rows_b"),
            (rows_b[64:65, 0:D], dr["lng"][None, :], "rows_b"),
            (gw_s[:], dr["gw"].rearrange("(c p) e -> p c e", p=128), "gw_s"),
            ]:
        P.op("sq", lambda s, d=dst, i=src: s.dma_start(out=d, in_=i), writes=[r], inc=16)
    for dst, srcd in [(biasrows[0:15, :], dr["rb1"][:]),
                      (biasrows[15:16, :], dr["sb1"][0:1, 0:FF]),
                      (biasrows[16:17, :], dr["sb1"][0:1, FF:2 * FF])]:
        P.op("gq", lambda g, d=dst, i=srcd: g.dma_start(out=d, in_=i), writes=["brow"], inc=16)

    def _brt(t):
        for c in range(NFF):
            ins = t.matmul(pC[:, c * 17:(c + 1) * 17],
                           lhsT=biasrows[0:17, c * 128:(c + 1) * 128],
                           rhs=idb[0:17, 0:17], start=True, stop=True)
        return ins
    P.op("pe", _brt, reads=["brow", "idb"], writes=["pC0"])
    P.op("dv", lambda v: v.tensor_copy(rb1T[:, 0:NFF * NE], pC[:, 0:NFF * NE]),
         reads=["pC0"], writes=["rb1T"])

    def w1_src(e):
        if e < 15:
            return dr["rw1"][e:e + 1].rearrange("o (c p) f -> p (o c) f", p=128)
        return sw1_2d[:, (e - 15) * FF:(e - 14) * FF].rearrange("(c p) f -> p c f", p=128)

    def w2_src(e):
        if e < 15:
            return dr["rw2"][e:e + 1].rearrange("o (c p) f -> p (o c) f", p=128)
        return sw2_2d[(e - 15) * FF:(e - 14) * FF, :].rearrange("(c p) f -> p c f", p=128)

    P.op("gq", lambda g: g.dma_start(out=rows_c[0:1, 0:D], in_=dr["bq"][None, :]), writes=["rows_c"], inc=16)
    P.op("gq", lambda g: g.dma_start(out=rows_c[32:33, 0:D], in_=dr["bv"][None, :]), writes=["rows_c"], inc=16)
    P.op("gq", lambda g: g.dma_start(out=rows_c[64:65, 0:D], in_=dr["bo"][None, :]), writes=["rows_c"], inc=16)
    P.op("gq", lambda g: g.dma_start(out=wA[:], in_=w1_src(0)), writes=["wA"], inc=16)
    P.op("gq", lambda g: g.dma_start(out=wB[:], in_=w2_src(0)), writes=["wB"], inc=16)

    P.barrier()
    ctxb.close()

    if stop_after == "startup":
        P.final_wait("sq")
        P.replay(nc, sems)
        ctx2.close(); ctxa.close(); ctx.close()
        return _stage_and_run(nc, inputs, core_ids, trace)

    nt_lim = NT if stop_after != "tiles1" else 1

    gsc = [p0scr[:, 0:2048].bitcast(F32), p0scr[:, 2048:4096].bitcast(F32)]

    def emit_rb2row(e):
        if e < 16:
            rsrc = dr["rb2"][e:e + 1, :] if e < 15 else dr["sb2"][0:1, :]
            P.op("gq", lambda g, rsrc=rsrc: g.dma_start(out=rb2row[:], in_=rsrc),
                 writes=["rb2row"], inc=16)

    def emit_wprefetch(e):
        if e + 1 < NE:
            P.op("gq", lambda g, e=e: g.dma_start(out=wA[:], in_=w1_src(e + 1)),
                 writes=["wA"], inc=16)
            P.op("gq", lambda g, e=e: g.dma_start(out=wB[:], in_=w2_src(e + 1)),
                 writes=["wB"], inc=16)

    def emit_eq(e, q):
        if True:
            for fi, f in enumerate([15] + list(range(15))):
                ph = pA[:, 0:512] if fi % 2 == 0 else pA[:, 512:1024]
                phr = "pA0" if fi % 2 == 0 else "pA1"
                def _m1(t, f=f, q=q, ph=ph):
                    for c in range(NC8):
                        ins = t.matmul(ph, lhsT=wA[:, c * FF + f * 128:c * FF + (f + 1) * 128],
                                       rhs=xnT[:, c * T + q * QT:c * T + (q + 1) * QT],
                                       start=(c == 0), stop=(c == NC8 - 1))
                    return ins
                P.op("pe", _m1, reads=["wA", "xnT"], writes=[phr])
                P.op("ac", lambda s, f=f, e=e, ph=ph: s.activation(
                        hbuf[:, f * QT:(f + 1) * QT], ph, AF.Gelu,
                        bias=rb1T[:, f * NE + e:f * NE + e + 1], scale=1.0),
                     reads=[phr, "rb1T"], writes=[f"hb{f // 4}"])
            for j2 in range(4):
                J = q * 4 + j2
                pm = pC if J % 2 == 0 else pD
                pmr = None
                def _m2a(t, j2=j2, pm=pm):
                    for cf in range(NFF - 1, 7, -1):
                        for s_ in range(2):
                            ins = t.matmul(pm[:, s_ * 512:(s_ + 1) * 512],
                                           lhsT=hbuf[:, cf * QT + j2 * 128:cf * QT + (j2 + 1) * 128],
                                           rhs=wB[:, cf * D + s_ * 512:cf * D + (s_ + 1) * 512],
                                           start=(cf == NFF - 1), stop=False)
                    return ins
                def _m2b(t, j2=j2, pm=pm, e=e):
                    for cf in range(7, -1, -1):
                        for s_ in range(2):
                            ins = t.matmul(pm[:, s_ * 512:(s_ + 1) * 512],
                                           lhsT=hbuf[:, cf * QT + j2 * 128:cf * QT + (j2 + 1) * 128],
                                           rhs=wB[:, cf * D + s_ * 512:cf * D + (s_ + 1) * 512],
                                           start=False, stop=(cf == 0 and e >= 16))
                    if e < 16:
                        for s_ in range(2):
                            ins = t.matmul(pm[:, s_ * 512:(s_ + 1) * 512], lhsT=onesbr[0:1, :],
                                           rhs=rb2row[0:1, s_ * 512:(s_ + 1) * 512],
                                           start=False, stop=True)
                    return ins
                pmrs = ["pC0", "pC1"] if J % 2 == 0 else ["pD0", "pD1"]
                mm2a_reads = ["hb2", "hb3", "wB"]
                mm2b_reads = ["hb0", "hb1", "wB", "onesbr"]
                if e < 16:
                    mm2b_reads.append("rb2row")
                P.op("pe", _m2a, reads=mm2a_reads, writes=pmrs)
                P.op("pe", _m2b, reads=mm2b_reads, writes=pmrs)
                a_J = acc[:, J * D:(J + 1) * D]
                if e < 15:
                    P.op("dv", lambda v, a=a_J, J=J, e=e, pm=pm: v.scalar_tensor_tensor(
                            a, pm[:], gated[:, J * 15 + e:J * 15 + e + 1], a,
                            op0=ALU.mult, op1=ALU.add),
                         reads=pmrs + ["gated", f"acc{J % 4}"], writes=[f"acc{J % 4}"])
                else:
                    P.op("dv", lambda v, a=a_J, pm=pm: v.tensor_tensor(a, a, pm[:], op=ALU.add),
                         reads=pmrs + [f"acc{J % 4}"], writes=[f"acc{J % 4}"])
                if e == NE - 1:
                    gv = gsc[J % 2]
                    gr = f"p0g{J % 2}"
                    P.op("ac", lambda s, J=J, gv=gv: s.activation(gv, acc[:, J * D:(J + 1) * D],
                                                                  AF.Silu, bias=Z, scale=1.0),
                         reads=[f"acc{J % 4}", "smal"],
                         writes=[gr, "xb", "xbfr", "xnT32"])
                    P.op("gq", lambda g, J=J, gv=gv: g.dma_start(
                            out=g_dram[J * 128:(J + 1) * 128, :], in_=gv),
                         reads=[gr], writes=[f"gd{J}"], inc=16)



    for j in range(nt_lim):
        xb = xbuf[0]
        xr = "xb"
        a_j = acc[:, j * D:(j + 1) * D]
        ar = f"acc{j % 4}"
        sb0 = 12 * (j % 2)
        sr = f"st{j % 2}"
        pTR = pA if j % 2 == 0 else pD
        pTRr = ["pA0", "pA1"] if j % 2 == 0 else ["pD0", "pD1"]
        pXR = pD if j % 2 == 0 else pA
        pXRr = ["pD0", "pD1"] if j % 2 == 0 else ["pA0", "pA1"]
        pG = pC[:, 0:15] if j % 2 == 0 else pC[:, 512:527]
        pGr = "pC0" if j % 2 == 0 else "pC1"
        P.op("sq", lambda s, xb=xb, j=j: s.dma_start(out=xb, in_=dr["xin"][j * 128:(j + 1) * 128, :]),
             writes=[xr], inc=16)
        P.op("dv", lambda v, xb=xb, sb0=sb0: v.reduce_sum(st[:, sb0:sb0 + 1], xb, axis=AX.X),
             reads=[xr], writes=[sr])
        P.op("ac", lambda s, xb=xb, sb0=sb0: s.activation(sqout, xb, AF.Square, bias=Z, scale=1.0,
                                                 accum_out=st[:, sb0 + 2:sb0 + 3]),
             reads=[xr, "smal"], writes=["xbfr", sr])
        P.op("dv", lambda v, sb0=sb0: v.tensor_scalar_mul(st[:, sb0 + 1:sb0 + 2], st[:, sb0:sb0 + 1], 1.0 / D),
             reads=[sr], writes=[sr])
        P.op("dv", lambda v, sb0=sb0: v.tensor_tensor(st[:, sb0 + 5:sb0 + 6], st[:, sb0 + 1:sb0 + 2],
                                                      st[:, sb0 + 1:sb0 + 2], op=ALU.mult),
             reads=[sr], writes=[sr])
        P.op("dv", lambda v, sb0=sb0: v.scalar_tensor_tensor(st[:, sb0 + 6:sb0 + 7], st[:, sb0 + 2:sb0 + 3],
                                                             1.0 / D, st[:, sb0 + 5:sb0 + 6],
                                                             op0=ALU.mult, op1=ALU.subtract),
             reads=[sr], writes=[sr])
        P.op("ac", lambda s, sb0=sb0: s.activation(st[:, sb0 + 3:sb0 + 4], st[:, sb0 + 6:sb0 + 7], AF.Ln, bias=EPS, scale=1.0),
             reads=[sr, "smal"], writes=[sr])
        P.op("ac", lambda s, sb0=sb0: s.activation(st[:, sb0 + 4:sb0 + 5], st[:, sb0 + 3:sb0 + 4], AF.Exp, bias=Z, scale=-0.5),
             reads=[sr, "smal"], writes=[sr])
        P.op("dv", lambda v, sb0=sb0: v.tensor_tensor(st[:, sb0 + 7:sb0 + 8], st[:, sb0 + 1:sb0 + 2],
                                                      st[:, sb0 + 4:sb0 + 5], op=ALU.mult),
             reads=[sr], writes=[sr])
        P.op("dv", lambda v, sb0=sb0: v.tensor_scalar_mul(st[:, sb0 + 9:sb0 + 10], st[:, sb0 + 7:sb0 + 8], -1.0),
             reads=[sr], writes=[sr])
        P.op("ac", lambda s, xb=xb, a=a_j, sb0=sb0: s.activation(a, xb, AF.Identity,
                                                bias=st[:, sb0 + 9:sb0 + 10], scale=st[:, sb0 + 4:sb0 + 5]),
             reads=[xr, sr], writes=[ar])
        P.op("ac", lambda s, a=a_j: s.activation(xbf, a, AF.Identity, bias=Z, scale=1.0),
             reads=[ar, "smal"], writes=["xbfr"])
        def _tr(t, pTR=pTR):
            for c in range(NC8):
                ins = t.matmul(pTR[:, c * 128:(c + 1) * 128], lhsT=xbf[:, c * 128:(c + 1) * 128],
                               rhs=idb[:], start=True, stop=True)
            return ins
        P.op("pe", _tr, reads=["xbfr", "idb"], writes=pTRr)
        xnT_view = xnT.ap().rearrange("p (c t) -> p c t", c=NC8)[:, :, j * 128:(j + 1) * 128]
        P.op("ac", lambda s, v_=xnT_view, pTR=pTR: s.activation(v_, pTR[:], AF.Identity, bias=Z, scale=1.0),
             reads=pTRr + ["smal"], writes=["xnT"])

        def _xtr32(t, j=j, pXR=pXR):
            for c in range(NC8):
                ins = t.matmul(pXR[:, c * 128:(c + 1) * 128],
                               lhsT=acc[:, j * D + c * 128:j * D + (c + 1) * 128],
                               rhs=idf[:], start=True, stop=True)
            return ins
        P.op("pe", _xtr32, reads=[ar, "idf"], writes=pXRr)
        P.op("dv", lambda v, pXR=pXR: v.tensor_copy(xnT32[:], pXR[:]), reads=pXRr, writes=["xnT32"])
        def _gmm(t, pG=pG):
            for c in range(NC8):
                t.matmul(pG, lhsT=xnT32[:, c * 128:(c + 1) * 128],
                         rhs=gw_s[:, c * 15:(c + 1) * 15], start=(c == 0), stop=False)
            return t.matmul(pG, lhsT=onesf[0:1, :], rhs=rows_a[0:1, D:D + 15],
                            start=False, stop=True)
        P.op("pe", _gmm, reads=["xnT32", "gw_s", "rows_a", "onesf"], writes=[pGr])
        P.op("dv", lambda v, pG=pG, sb0=sb0: v.reduce_max(st[:, sb0 + 5:sb0 + 6], pG, axis=AX.X),
             reads=[pGr], writes=[sr])
        P.op("dv", lambda v, sb0=sb0: v.tensor_scalar_mul(st[:, sb0 + 6:sb0 + 7], st[:, sb0 + 5:sb0 + 6], -1.0),
             reads=[sr], writes=[sr])
        P.op("ac", lambda s, pG=pG, sb0=sb0: s.activation(w15[:], pG, AF.Exp, bias=st[:, sb0 + 6:sb0 + 7], scale=1.0,
                                          accum_out=st[:, sb0 + 7:sb0 + 8]),
             reads=[pGr, sr], writes=["w15", sr])
        P.op("dv", lambda v, sb0=sb0: v.reciprocal(st[:, sb0 + 8:sb0 + 9], st[:, sb0 + 7:sb0 + 8]),
             reads=[sr], writes=[sr])
        P.op("dv", lambda v, sb0=sb0: v.tensor_scalar(w15[:], w15[:], st[:, sb0 + 8:sb0 + 9], None, op0=ALU.mult),
             reads=["w15", sr], writes=["w15"])
        P.op("dv", lambda v: v.max(out=mx8[:], in_=w15[:]), reads=["w15"], writes=["mx8"])
        P.op("dv", lambda v: v.memset(mx8[:, 3:8], -1e30), writes=["mx8"])
        P.op("dv", lambda v: v.match_replace(out=wzz[:], in_to_replace=mx8[:], in_values=w15[:],
                                             imm_value=-1e30),
             reads=["mx8", "w15"], writes=["wzz", "mx8"])
        P.op("dv", lambda v: v.tensor_tensor(wzm[:], w15[:], wzz[:], op=ALU.not_equal),
             reads=["w15", "wzz"], writes=["wzm"])
        P.op("dv", lambda v, j=j: v.tensor_tensor(gated[:, j * 15:(j + 1) * 15], wzm[:], w15[:],
                                                  op=ALU.mult),
             reads=["wzm", "w15"], writes=["gated"])

        if j % 4 == 3 and stop_after != "p1":
            if j == 3:
                emit_rb2row(0)
            emit_eq(0, j // 4)
            if j == 15:
                emit_wprefetch(0)

    tap("xnT", lambda: xnT[:], ["xnT"])
    tap("acc1", lambda: acc[:], ["acc0", "acc1", "acc2", "acc3"])
    tap("gated", lambda: gated[:], ["gated"])
    if stop_after == "tiles1":
        P.final_wait("sq")
        P.replay(nc, sems)
        ctx2.close(); ctxa.close(); ctx.close()
        return _stage_and_run(nc, inputs, core_ids, trace)

    for e in (range(1, NE) if stop_after != "p1" else []):
        emit_rb2row(e)
        for q in range(NQ):
            emit_eq(e, q)
        emit_wprefetch(e)

    tap("accf", lambda: acc[:], ["acc0", "acc1", "acc2", "acc3"])

    if stop_after in ("p1", "p2"):
        P.final_wait("sq")
        P.replay(nc, sems)
        ctxb.close(); ctx2.close(); ctxa.close(); ctx.close()
        return _stage_and_run(nc, inputs, core_ids, trace)

    P.barrier()
    ctx2.close()

    ctx3 = ExitStack()
    khT = sb_(ctx3, "khT", [128, NC8 * T], BF16, side="right")
    vtm = sb_(ctx3, "vtm", [128, NT * 1040], BF16, side="right")
    wq_s = sb_(ctx3, "wq_s", [128, NC8 * D], BF16, side="right")
    wo_s = sb_(ctx3, "wo_s", [128, NC8 * D], BF16, side="right")
    bkT = sb_(ctx3, "bkT", [128, 8], F32, side="right")
    psS = ctx3.enter_context(nc.psum_tensor("psS", [128, 2048], F32))
    psQ = ctx3.enter_context(nc.psum_tensor("psQ", [128, 1024], F32))
    psT = ctx3.enter_context(nc.psum_tensor("psT", [128, 1024], F32))
    sunit = [psS[:, 0:1024], psS[:, 1024:2048], psT[:, 0:1024]]

    P.op("gq", lambda g: g.dma_start(out=wq_s[:], in_=dr["wk"].rearrange("(c p) f -> p c f", p=128)),
         writes=["wq_s"], inc=16)
    P.op("gq", lambda g: g.dma_start(out=wo_s[:], in_=dr["wv"].rearrange("(c p) f -> p c f", p=128)),
         writes=["wo_s"], inc=16)

    def _bkt(t):
        for c in range(NC8):
            ins = t.matmul(psT[:, c:c + 1], lhsT=rows_b[0:1, c * 128:(c + 1) * 128],
                           rhs=onesf[0:1, 0:1], start=True, stop=True)
        return ins
    P.op("pe", _bkt, reads=["rows_b", "onesf"], writes=["psU2"])
    P.op("dv", lambda v: v.tensor_copy(bkT[:], psT[:, 0:8]), reads=["psU2"], writes=["bkT"])

    for cof in range(NC8):
        for hf in range(2):
            flip = (cof * 2 + hf) % 2 == 0
            pk = psQ if flip else psT
            pkw = ["psQa", "psQb"] if flip else ["psU2"]
            def _kmm(t, cof=cof, hf=hf, pk=pk):
                for c in range(NC8):
                    for s_ in range(2):
                        ins = t.matmul(
                            pk[:, s_ * 512:(s_ + 1) * 512],
                            lhsT=wq_s[:, c * D + cof * 128:c * D + (cof + 1) * 128],
                            rhs=xnT[:, c * T + hf * 1024 + s_ * 512:c * T + hf * 1024 + (s_ + 1) * 512],
                            start=(c == 0), stop=(c == NC8 - 1))
                return ins
            P.op("pe", _kmm, reads=["wq_s", "xnT"], writes=pkw)
            P.op("ac", lambda s, cof=cof, hf=hf, pk=pk: s.activation(
                    khT[:, cof * T + hf * 1024:cof * T + (hf + 1) * 1024], pk[:], AF.Identity,
                    bias=bkT[:, cof:cof + 1], scale=1.0),
                 reads=pkw + ["bkT"], writes=["khT"])
    P.op("dv", lambda v: v.memset(vtm[:], 1.0), writes=["vtm"])
    for kt in range(NT):
        flip = kt % 2 == 0
        pk = psQ if flip else psT
        pkw = ["psQa", "psQb"] if flip else ["psU2"]
        def _vmm(t, kt=kt, pk=pk):
            for s_ in range(2):
                for c in range(NC8):
                    t.matmul(pk[:, s_ * 512:(s_ + 1) * 512],
                             lhsT=xnT[:, c * T + kt * 128:c * T + (kt + 1) * 128],
                             rhs=wo_s[:, c * D + s_ * 512:c * D + (s_ + 1) * 512],
                             start=(c == 0), stop=False)
                ins = t.matmul(pk[:, s_ * 512:(s_ + 1) * 512], lhsT=onesbr[32:33, :],
                               rhs=rows_c[32:33, s_ * 512:(s_ + 1) * 512], start=False, stop=True)
            return ins
        P.op("pe", _vmm, reads=["xnT", "wo_s", "rows_c", "onesbr"], writes=pkw)
        vv = vtm.ap().rearrange("p (kt h e) -> p kt h e", kt=NT, h=H)[:, kt, :, 0:64]
        if kt % 2 == 0:
            P.op("ac", lambda s, vv=vv, pk=pk: s.activation(vv, pk[:], AF.Identity, bias=Z, scale=1.0),
                 reads=pkw + ["smal"], writes=["vtm"])
        else:
            P.op("dv", lambda v, vv=vv, pk=pk: v.tensor_copy(vv, pk[:]),
                 reads=pkw, writes=["vtm"])

    P.op("gq", lambda g: g.dma_start(out=wq_s[:], in_=dr["wq"].rearrange("(c p) f -> p c f", p=128)),
         writes=["wq_s"], inc=16)
    P.op("gq", lambda g: g.dma_start(out=wo_s[:], in_=dr["wo"].rearrange("(c p) f -> p c f", p=128)),
         writes=["wo_s"], inc=16)

    tap("khT", lambda: khT[:], ["khT"])
    tap("vtm", lambda: vtm[:], ["vtm"])

    if stop_after == "p3":
        P.final_wait("sq")
        P.replay(nc, sems)
        ctxb.close(); ctx3.close(); ctxa.close(); ctx.close()
        return _stage_and_run(nc, inputs, core_ids, trace)

    P.barrier()
    ctxa.close()

    ctx4 = ExitStack()
    qnP = [sb_(ctx4, "qn0t", [128, D], F32), sb_(ctx4, "qn1t", [128, D], F32)]
    qnbP = [sb_(ctx4, "qnb0", [128, D], BF16), sb_(ctx4, "qnb1", [128, D], BF16)]
    qTbP = [sb_(ctx4, "qTb0", [128, D], BF16), sb_(ctx4, "qTb1", [128, D], BF16)]
    expST = sb_(ctx4, "expST", [128, 4096], BF16)
    snk = sb_(ctx4, "snk", [128, D], BF16)
    ao = sb_(ctx4, "ao", [128, D], F32)
    abf = sb_(ctx4, "abf", [128, D], BF16)
    aTb = sb_(ctx4, "aTb", [128, D], BF16)
    xg = sb_(ctx4, "xg", [128, 2 * D], F32)
    ob = sb_(ctx4, "ob", [128, D], F32)
    gbc = sb_(ctx4, "gbc", [128, D], F32)
    bbc = sb_(ctx4, "bbc", [128, D], F32)
    mh = sb_(ctx4, "mh", [128, 16], F32)
    qst = sb_(ctx4, "qst", [128, 16], F32)

    def _gbcmm(t):
        for s_ in range(2):
            ins = t.matmul(psQ[:, s_ * 512:(s_ + 1) * 512], lhsT=onesf[64:65, :],
                           rhs=rows_b[64:65, s_ * 512:(s_ + 1) * 512], start=True, stop=True)
        return ins
    P.op("pe", _gbcmm, reads=["rows_b", "onesf"], writes=["psQa", "psQb"])
    P.op("dv", lambda v: v.tensor_copy(gbc[:], psQ[:]), reads=["psQa", "psQb"], writes=["gbc"])
    def _bbcmm(t):
        for s_ in range(2):
            ins = t.matmul(psQ[:, s_ * 512:(s_ + 1) * 512], lhsT=onesf[0:1, :],
                           rhs=rows_a[0:1, s_ * 512:(s_ + 1) * 512], start=True, stop=True)
        return ins
    P.op("pe", _bbcmm, reads=["rows_a", "onesf"], writes=["psQa", "psQb"])
    P.op("dv", lambda v: v.tensor_copy(bbc[:], psQ[:]), reads=["psQa", "psQb"], writes=["bbc"])

    unext = [0]
    def take_unit():
        u = unext[0] % 3
        unext[0] += 1
        return sunit[u], f"psU{u}"

    def emit_qmm_ln(jj):
        pp = jj % 2
        qn_, qnb_ = qnP[pp], qnbP[pp]
        rq, rqb = f"qn{pp}", f"qnb{pp}"
        sq0 = 8 * pp
        rst = f"qst{pp}"
        pS, rS = take_unit()
        def _qmm(t, jj=jj, pS=pS):
            for s_ in range(2):
                for c in range(NC8):
                    t.matmul(pS[:, s_ * 512:(s_ + 1) * 512],
                             lhsT=xnT[:, c * T + jj * 128:c * T + (jj + 1) * 128],
                             rhs=wq_s[:, c * D + s_ * 512:c * D + (s_ + 1) * 512],
                             start=(c == 0), stop=False)
                ins = t.matmul(pS[:, s_ * 512:(s_ + 1) * 512], lhsT=onesbr[0:1, :],
                               rhs=rows_c[0:1, s_ * 512:(s_ + 1) * 512], start=False, stop=True)
            return ins
        P.op("pe", _qmm, reads=["xnT", "wq_s", "rows_c", "onesbr"], writes=[rS])
        P.op("dv", lambda v, pS=pS, sq0=sq0: v.reduce_sum(qst[:, sq0:sq0 + 1], pS[:], axis=AX.X),
             reads=[rS], writes=[rst])
        P.op("ac", lambda s, pS=pS, sq0=sq0: s.activation(snk[:], pS[:], AF.Square, bias=Z, scale=1.0,
                                                          accum_out=qst[:, sq0 + 1:sq0 + 2]),
             reads=[rS, "smal"], writes=["snk", rst])
        P.op("dv", lambda v, sq0=sq0: v.tensor_scalar_mul(qst[:, sq0 + 2:sq0 + 3], qst[:, sq0:sq0 + 1], 1.0 / D),
             reads=[rst], writes=[rst])
        P.op("dv", lambda v, sq0=sq0: v.tensor_tensor(qst[:, sq0 + 3:sq0 + 4], qst[:, sq0 + 2:sq0 + 3],
                                                      qst[:, sq0 + 2:sq0 + 3], op=ALU.mult),
             reads=[rst], writes=[rst])
        P.op("dv", lambda v, sq0=sq0: v.scalar_tensor_tensor(qst[:, sq0 + 4:sq0 + 5], qst[:, sq0 + 1:sq0 + 2],
                                                             1.0 / D, qst[:, sq0 + 3:sq0 + 4],
                                                             op0=ALU.mult, op1=ALU.subtract),
             reads=[rst], writes=[rst])
        P.op("ac", lambda s, sq0=sq0: s.activation(qst[:, sq0 + 5:sq0 + 6], qst[:, sq0 + 4:sq0 + 5],
                                                   AF.Ln, bias=EPS, scale=1.0),
             reads=[rst, "smal"], writes=[rst])
        P.op("ac", lambda s, sq0=sq0: s.activation(qst[:, sq0 + 6:sq0 + 7], qst[:, sq0 + 5:sq0 + 6],
                                                   AF.Exp, bias=Z, scale=-0.5),
             reads=[rst, "smal"], writes=[rst])
        P.op("dv", lambda v, sq0=sq0: v.tensor_tensor(qst[:, sq0 + 7:sq0 + 8], qst[:, sq0 + 2:sq0 + 3],
                                                      qst[:, sq0 + 6:sq0 + 7], op=ALU.mult),
             reads=[rst], writes=[rst])
        P.op("dv", lambda v, sq0=sq0: v.tensor_scalar_mul(qst[:, sq0 + 7:sq0 + 8], qst[:, sq0 + 7:sq0 + 8], -1.0),
             reads=[rst], writes=[rst])
        P.op("ac", lambda s, qn_=qn_, pS=pS, sq0=sq0: s.activation(
                qn_[:], pS[:], AF.Identity, bias=qst[:, sq0 + 7:sq0 + 8], scale=qst[:, sq0 + 6:sq0 + 7]),
             reads=[rS, rst], writes=[rq])
        P.op("dv", lambda v, qn_=qn_: v.tensor_tensor(qn_[:], qn_[:], gbc[:], op=ALU.mult),
             reads=[rq, "gbc"], writes=[rq])
        P.op("dv", lambda v, qn_=qn_: v.tensor_tensor(qn_[:], qn_[:], bbc[:], op=ALU.add),
             reads=[rq, "bbc"], writes=[rq])
        P.op("dv", lambda v, qn_=qn_, qnb_=qnb_: v.tensor_copy(qnb_[:], qn_[:]), reads=[rq], writes=[rqb])

    def emit_qtr(jj):
        pp = jj % 2
        qnb_, qTb_ = qnbP[pp], qTbP[pp]
        rqb, rqt = f"qnb{pp}", f"qTb{pp}"
        pS, rS = take_unit()
        def _qtr(t, pS=pS, qnb_=qnb_):
            for c in range(NC8):
                ins = t.matmul(pS[:, c * 128:(c + 1) * 128], lhsT=qnb_[:, c * 128:(c + 1) * 128],
                               rhs=idb[:], start=True, stop=True)
            return ins
        P.op("pe", _qtr, reads=[rqb, "idb"], writes=[rS])
        P.op("dv", lambda v, pS=pS, qTb_=qTb_: v.tensor_copy(qTb_[:], pS[:]), reads=[rS], writes=[rqt])

    def emit_av(h):
        par = h % 2
        avs = psQ[:, 0:65] if par == 0 else psQ[:, 512:577]
        avr = "psQa" if par == 0 else "psQb"
        def _av(t, h=h, avs=avs, par=par):
            for kt in range(NT):
                ins = t.matmul(avs, lhsT=expST[:, par * 2048 + kt * 128:par * 2048 + (kt + 1) * 128],
                               rhs=vtm[:, kt * 1040 + h * 65:kt * 1040 + (h + 1) * 65],
                               start=(kt == 0), stop=(kt == NT - 1))
            return ins
        P.op("pe", _av, reads=["expST" + ("a" if par == 0 else "b"), "vtm"], writes=[avr])
        mhr = f"mh{par}"
        P.op("dv", lambda v, h=h, avs=avs: v.tensor_scalar(mh[:, 4 + (h % 2):5 + (h % 2)],
                                                           avs[:, 64:65], 1e-8, None, op0=ALU.add),
             reads=[avr], writes=[mhr])
        P.op("dv", lambda v, h=h: v.reciprocal(mh[:, 8 + (h % 2):9 + (h % 2)],
                                               mh[:, 4 + (h % 2):5 + (h % 2)]),
             reads=[mhr], writes=[mhr])
        P.op("dv", lambda v, h=h, avs=avs: v.tensor_scalar(
                ao[:, h * 64:(h + 1) * 64], avs[:, 0:64],
                mh[:, 8 + (h % 2):9 + (h % 2)], None, op0=ALU.mult),
             reads=[avr, mhr], writes=["ao"])

    emit_qmm_ln(0)
    emit_qtr(0)
    for j in range(NT):
        pp = j % 2
        qn_, qTb_ = qnP[pp], qTbP[pp]
        rq, rqt = f"qn{pp}", f"qTb{pp}"
        xv = xg[:, 0:D]
        gv4 = xg[:, D:2 * D]
        xrr, grr = "x4", "g4"
        P.op("sq", lambda s, j=j, xv=xv: s.dma_start(out=xv, in_=dr["xin"][j * 128:(j + 1) * 128, :]),
             writes=[xrr], inc=16)
        P.op("sq", lambda s, j=j, gv=gv4: s.dma_start(out=gv, in_=g_dram[j * 128:(j + 1) * 128, :]),
             reads=[f"gd{j}"], writes=[grr], inc=16)
        for h in range(H):
            poff = 64 * (h % 2)
            coff = (h // 2) * 128
            par = h % 2
            for kh in range(2):
                pS, rS = take_unit()
                def _stmm(t, poff=poff, coff=coff, h=h, kh=kh, pS=pS, qTb_=qTb_):
                    for kt in range(8):
                        ktg = kh * 8 + kt
                        ins = t.matmul(pS[:, kt * 128:(kt + 1) * 128],
                                       lhsT=khT[poff:poff + 64, (h // 2) * T + ktg * 128:(h // 2) * T + (ktg + 1) * 128],
                                       rhs=qTb_[poff:poff + 64, coff:coff + 128],
                                       start=True, stop=True)
                    return ins
                P.op("pe", _stmm, reads=[rqt, "khT"], writes=[rS])
                P.op("ac", lambda s, par=par, kh=kh, pS=pS: s.activation(
                        expST[:, par * 2048 + kh * 1024:par * 2048 + (kh + 1) * 1024], pS[:],
                        AF.Exp, bias=Z, scale=0.125),
                     reads=[rS, "smal"], writes=["expST" + ("a" if par == 0 else "b")])
            if h > 0:
                emit_av(h - 1)
            if h == 7 and j + 1 < NT:
                emit_qmm_ln(j + 1)
            if h == 11 and j + 1 < NT:
                emit_qtr(j + 1)
        emit_av(H - 1)
        P.op("dv", lambda v, qn_=qn_: v.tensor_tensor(abf[:], ao[:], qn_[:], op=ALU.add),
             reads=["ao", rq], writes=["abf"])
        pS, rS = take_unit()
        def _atr(t, pS=pS):
            for c in range(NC8):
                ins = t.matmul(pS[:, c * 128:(c + 1) * 128], lhsT=abf[:, c * 128:(c + 1) * 128],
                               rhs=idb[:], start=True, stop=True)
            return ins
        P.op("pe", _atr, reads=["abf", "idb"], writes=[rS])
        P.op("dv", lambda v, pS=pS: v.tensor_copy(aTb[:], pS[:]), reads=[rS], writes=["aTb"])
        def _omm(t):
            for s_ in range(2):
                for c in range(NC8):
                    t.matmul(psQ[:, s_ * 512:(s_ + 1) * 512], lhsT=aTb[:, c * 128:(c + 1) * 128],
                             rhs=wo_s[:, c * D + s_ * 512:c * D + (s_ + 1) * 512],
                             start=(c == 0), stop=False)
                ins = t.matmul(psQ[:, s_ * 512:(s_ + 1) * 512], lhsT=onesbr[64:65, :],
                               rhs=rows_c[64:65, s_ * 512:(s_ + 1) * 512], start=False, stop=True)
            return ins
        P.op("pe", _omm, reads=["aTb", "wo_s", "rows_c", "onesbr"], writes=["psQa", "psQb"])
        P.op("dv", lambda v, gv=gv4: v.tensor_tensor(ob[:], psQ[:], gv, op=ALU.mult),
             reads=["psQa", "psQb", grr], writes=["obr"])
        P.op("dv", lambda v, xv=xv: v.tensor_tensor(ob[:], ob[:], xv, op=ALU.add),
             reads=["obr", xrr], writes=["obr"])
        P.op("sq", lambda s, j=j: s.dma_start(out=out_d[j * 128:(j + 1) * 128, :], in_=ob[:]),
             reads=["obr"], inc=16)
        if j == 0:
            tap("qn0", lambda: qnP[0][:], ["qn0"])
            tap("ao0", lambda: ao[:], ["ao"])

    P.final_wait("sq")
    P.replay(nc, sems)
    ctx4.close()
    ctx3.close()
    ctx.close()
    return _stage_and_run(nc, inputs, core_ids, trace)


def _stage_and_run(nc, inputs, core_ids, trace):
    x_full = np.ascontiguousarray(np.asarray(inputs["L0_attn_x"], dtype=np.float32))
    base = {
        "wq": inputs["L0_attn_Wq"], "bq": inputs["L0_attn_bq"],
        "wk": inputs["L0_attn_Wk"], "bk": inputs["L0_attn_bk"],
        "wv": inputs["L0_attn_Wv"], "bv": inputs["L0_attn_bv"],
        "wo": inputs["L0_attn_Wo"], "bo": inputs["L0_attn_bo"],
        "lng": inputs["L0_attn_ln_g"], "lnb": inputs["L0_attn_ln_b"],
        "sw1": inputs["L0_ffn_sW1"], "sb1": inputs["L0_ffn_sb1"],
        "sw2": inputs["L0_ffn_sW2"], "sb2": inputs["L0_ffn_sb2"],
        "rw1": inputs["L0_ffn_rW1"], "rb1": inputs["L0_ffn_rb1"],
        "rw2": inputs["L0_ffn_rW2"], "rb2": inputs["L0_ffn_rb2"],
        "gw": inputs["L0_ffn_gW"], "gb": inputs["L0_ffn_gb"],
    }
    base = {k: np.ascontiguousarray(np.asarray(v, dtype=np.float32)) for k, v in base.items()}
    base["idb"] = np.eye(128, dtype=BF)
    base["idf"] = np.eye(128, dtype=np.float32)
    in_maps = []
    for c in range(len(core_ids)):
        m = dict(base)
        m["xin"] = np.ascontiguousarray(x_full[c])
        in_maps.append(m)
    return run_bass_kernel_spmd(nc, in_maps, core_ids=core_ids, trace=trace)


def kernel(**inputs):
    res = build_and_run(inputs)
    return np.stack([r["xout"] for r in res.results], axis=0).astype(np.float32)
